# version 02
# baseline (speedup 1.0000x reference)
import numpy as np
from contextlib import ExitStack

import concourse.bass as bass
import concourse.mybir as mybir
from concourse.bass_utils import run_bass_kernel_spmd

F32 = mybir.dt.float32
BF16 = mybir.dt.bfloat16
FP8 = mybir.dt.float8e4
AF = mybir.ActivationFunctionType
ALU = mybir.AluOpType
AX = mybir.AxisListType

B, S, D, H = 4, 2048, 2048, 16
HID = 8192
NC = 16
NH = 64
NT = 16
NOWN = 1024
S2 = float(1.0 / np.sqrt(128.0))
EPS = 1e-5

ENGINES = ("T", "V", "S", "SP", "G")


class Op:
    __slots__ = ("eng", "emit", "deps", "inc", "sem_key", "needs_inc", "sem_val")

    def __init__(self, eng, emit, deps, inc, sem_key):
        self.eng = eng
        self.emit = emit
        self.deps = [d for d in deps if d is not None]
        self.inc = inc
        self.sem_key = sem_key
        self.needs_inc = False
        self.sem_val = None


class Prog:
    def __init__(self):
        self.eng_ops = {e: [] for e in ENGINES}
        self.last_dma = {}

    def op(self, eng, emit, deps=(), inc=1, sem_key=None):
        o = Op(eng, emit, deps, inc, sem_key or eng)
        if inc == 16:
            prev = self.last_dma.get(o.sem_key)
            if prev is not None:
                o.deps.append(prev)
            self.last_dma[o.sem_key] = o
        self.eng_ops[eng].append(o)
        return o

    def dma(self, eng, emit, deps=(), sem_key=None):
        assert sem_key is not None
        return self.op(eng, emit, deps, inc=16, sem_key=sem_key)

    def lower(self):
        for e in ENGINES:
            for o in self.eng_ops[e]:
                for d in o.deps:
                    if d.eng != o.eng or d.inc == 16:
                        d.needs_inc = True
        counts = {}
        keys = set()
        for e in ENGINES:
            for o in self.eng_ops[e]:
                if o.inc == 16:
                    o.needs_inc = True
                if o.needs_inc:
                    counts[o.sem_key] = counts.get(o.sem_key, 0) + o.inc
                    o.sem_val = counts[o.sem_key]
                    keys.add(o.sem_key)
        self.final_counts = counts
        return keys

    def emit_engine(self, eng_name, handle, sems):
        wm = {}
        for o in self.eng_ops[eng_name]:
            for d in o.deps:
                if d.eng == o.eng and d.inc != 16:
                    continue
                assert d.sem_val is not None, f"dep missing inc {d.eng}/{d.sem_key}"
                if wm.get(d.sem_key, -1) >= d.sem_val:
                    continue
                handle.wait_ge(sems[d.sem_key], d.sem_val)
                wm[d.sem_key] = d.sem_val
            inst = o.emit(handle)
            if o.needs_inc:
                if inst is None:
                    inst = handle.engine_nop()
                inst.then_inc(sems[o.sem_key], o.inc)


class Res:

    def __init__(self, seed_readers=()):
        self.writer = None
        self.readers = [r for r in seed_readers if r is not None]

    def write_deps(self):
        return list(self.readers) if self.readers else ([self.writer] if self.writer else [])

    def wrote(self, op):
        self.writer = op
        self.readers = []

    def touch(self, op):
        self.writer = op

    def read(self, op):
        if op is not None:
            self.readers.append(op)
        return op


def ts(i, n=128):
    return slice(i * n, (i + 1) * n)


def build_program(n_heads=H, stage="full", dbg_names=()):
    nc = bass.Bass("TRN2", target_bir_lowering=False)

    xt = nc.dram_tensor("xt", [D, S], F32, kind="ExternalInput")
    wq4 = nc.dram_tensor("wq4", [H, 128, D], F32, kind="ExternalInput")
    wk4 = nc.dram_tensor("wk4", [H, 128, D], F32, kind="ExternalInput")
    wv4 = nc.dram_tensor("wv4", [H, 128, D], F32, kind="ExternalInput")
    wo4 = nc.dram_tensor("wo4", [NC, 128, D], F32, kind="ExternalInput")
    bq = nc.dram_tensor("bq", [128, H], F32, kind="ExternalInput")
    bk = nc.dram_tensor("bk", [128, H], F32, kind="ExternalInput")
    bv = nc.dram_tensor("bv", [128, H], F32, kind="ExternalInput")
    bo = nc.dram_tensor("bo", [1, D], F32, kind="ExternalInput")
    wg0 = nc.dram_tensor("wg0", [NH, 128, D], F32, kind="ExternalInput")
    wg1 = nc.dram_tensor("wg1", [NH, 128, D], F32, kind="ExternalInput")
    wl = nc.dram_tensor("wl", [NH, 128, D], F32, kind="ExternalInput")
    pmb = nc.dram_tensor("pmb", [128, 1], F32, kind="ExternalInput")
    out = nc.dram_tensor("out", [D, NOWN], F32, kind="ExternalOutput")
    wl8 = nc.dram_tensor("wl8", [NH, 128, D], FP8)

    dbg = {}

    def dbg_t(name, shape, dtype):
        if name in dbg_names:
            dbg[name] = nc.dram_tensor("dbg_" + name, shape, dtype, kind="ExternalOutput")
        return dbg.get(name)

    p = Prog()
    est = ExitStack()
    sb = lambda name, shape, dtype, **kw: est.enter_context(nc.sbuf_tensor(name, shape, dtype, **kw))
    dbg_ops = []

    def dump(eng, dst, src_ap, deps):
        if dst is None:
            return None
        o = p.dma(eng, lambda q, a=dst, b=src_ap: q.dma_start(a[:, :], b), deps=deps, sem_key="dbg")
        dbg_ops.append(o)
        return o

    ident = sb("ident", [128, 128], BF16)
    ones_sq = sb("ones_sq", [128, 128], BF16)
    tri = sb("tri", [128, 128], BF16)
    ones_r = sb("ones_r", [1, 128], BF16)
    ones_c = sb("ones_c", [128, 1], BF16)
    ones_row = sb("ones_row", [1, 512], BF16)
    zcol = sb("zcol", [128, 1], F32)
    pmb_sb = sb("pmb_sb", [128, 1], F32)
    bq_sb = sb("bq_sb", [128, H], F32)
    bk_sb = sb("bk_sb", [128, H], F32)
    bv_sb = sb("bv_sb", [128, H], F32)
    bo_sb = sb("bo_sb", [1, D], BF16)

    boot = []
    boot.append(p.op("G", lambda g: g.memset(ident[:], 0.0)))
    boot.append(p.op("G", lambda g: g.affine_select(
        out=ident[:], in_=ident[:], compare_op=ALU.not_equal, fill=1.0,
        base=0, pattern=[[-1, 128]], channel_multiplier=1)))
    boot.append(p.op("G", lambda g: g.memset(tri[:], 1.0)))
    boot.append(p.op("G", lambda g: g.affine_select(
        out=tri[:], in_=tri[:], compare_op=ALU.is_ge, fill=0.0,
        base=0, pattern=[[1, 128]], channel_multiplier=-1)))
    boot.append(p.op("G", lambda g: g.memset(ones_sq[:], 1.0)))
    boot.append(p.op("G", lambda g: g.memset(ones_r[:], 1.0)))
    boot.append(p.op("G", lambda g: g.memset(ones_c[:], 1.0)))
    boot.append(p.op("G", lambda g: g.memset(ones_row[:], 1.0)))
    boot.append(p.op("G", lambda g: g.memset(zcol[:], 0.0)))
    lnc = sb("lnc", [1, 8], F32)
    boot.append(p.op("G", lambda g: g.memset(lnc[:], float(0.5 * np.log(25.0 / (128.0 * S))))))
    bias_ld = []
    bias_ld.append(p.dma("SP", lambda q: q.dma_start(pmb_sb[:], pmb[:, :]), sem_key="bias"))
    bias_ld.append(p.dma("SP", lambda q: q.dma_start(bq_sb[:], bq[:, :]), sem_key="bias"))
    bias_ld.append(p.dma("SP", lambda q: q.dma_start(bk_sb[:], bk[:, :]), sem_key="bias"))
    bias_ld.append(p.dma("SP", lambda q: q.dma_start(bv_sb[:], bv[:, :]), sem_key="bias"))
    bias_ld.append(p.dma("G", lambda q: q.dma_start(bo_sb[:], bo[:, :]), sem_key="biasg"))
    boot_all = boot + bias_ld

    oall_scope = ExitStack()
    o_all = oall_scope.enter_context(nc.sbuf_tensor("o_all", [128, H * NOWN], BF16))
    xn_scope = ExitStack()
    xn = xn_scope.enter_context(nc.sbuf_tensor("xn", [128, NC * S], BF16))

    ln_scope = ExitStack()
    xf32 = [ln_scope.enter_context(nc.sbuf_tensor(f"xf32_{i}", [128, NC * 512], F32)) for i in range(2)]
    _xbf1 = ln_scope.enter_context(nc.sbuf_tensor("xbf", [128, NC * 512], BF16))
    xbf = [_xbf1, _xbf1]
    sqbuf1 = ln_scope.enter_context(nc.sbuf_tensor("sqbuf1", [128, 2 * 512], BF16))
    lnb_m = ln_scope.enter_context(nc.sbuf_tensor("lnb_m", [128, 512], F32))
    lnb_a = ln_scope.enter_context(nc.sbuf_tensor("lnb_a", [128, 512], F32))
    lnb_b = ln_scope.enter_context(nc.sbuf_tensor("lnb_b", [128, 512], F32))
    tmpbuf = ln_scope.enter_context(nc.sbuf_tensor("tmpbuf", [128, 2 * 512], F32))

    lnp_scope = ExitStack()
    ps_sum = lnp_scope.enter_context(nc.psum_tensor("ps_sum", [128, 512], F32))
    ps_sq = lnp_scope.enter_context(nc.psum_tensor("ps_sq", [128, 512], F32))

    xslot = [Res(), Res()]
    xbslot = Res()
    ld_ops = {}

    def x_ap(tc):
        return bass.AP(xt, tc * 512, [[S, 128], [128 * S, NC], [1, 512]])

    def get_ldx(tc):
        if tc in ld_ops:
            return ld_ops[tc]
        par = tc % 2
        ldx = p.dma("SP", lambda q, a=xf32[par], b=x_ap(tc): q.dma_start(a[:, :], b),
                    deps=xslot[par].write_deps(), sem_key=f"x{par}")
        xslot[par].wrote(ldx)
        ld_ops[tc] = ldx
        return ldx

    def get_ldb(tc):
        ldb = p.dma("G", lambda q, b=x_ap(tc): q.dma_start(xbf[0][:, :], b),
                    deps=xbslot.write_deps() + (boot[:8] if tc == 0 else []),
                    sem_key="xb0")
        xbslot.wrote(ldb)
        return ldb

    ln1_chunks = []
    r_sum1, r_sq1 = Res(), Res()
    sq_slot1 = [Res(), Res()]
    tmp_slot1 = [Res(), Res()]
    r_lnb = Res()

    get_ldx(0)
    get_ldx(1)
    for tc in range(4):
        par = tc % 2
        ldx = get_ldx(tc)
        ldb = get_ldb(tc)
        gf = lambda di, par=par: xf32[par][:, ts(di, 512)]
        gb = lambda di: xbf[0][:, ts(di, 512)]
        sum_mm = []
        sq_mm = []
        for di in range(NC):
            sl = di % 2
            sq_op = p.op("V", lambda v, di=di, sl=sl, gb=gb: v.tensor_tensor(
                sqbuf1[:, ts(sl, 512)], gb(di), gb(di), ALU.mult),
                deps=[ldb] + sq_slot1[sl].write_deps())
            sq_slot1[sl].wrote(sq_op)
            xbslot.read(sq_op)
            m1 = p.op("T", lambda t, di=di, gb=gb: t.matmul(
                ps_sum[:, :], ones_sq[:], gb(di),
                start=(di == 0), stop=(di == NC - 1)),
                deps=([ldb] + r_sum1.write_deps()) if di == 0 else [])
            xbslot.read(m1)
            m2 = p.op("T", lambda t, di=di, sl=sl: t.matmul(
                ps_sq[:, :], ones_sq[:], sqbuf1[:, ts(sl, 512)],
                start=(di == 0), stop=(di == NC - 1)),
                deps=[sq_op] + (r_sq1.write_deps() if di == 0 else []))
            sq_slot1[sl].read(m2)
            sum_mm.append(m1)
            sq_mm.append(m2)
        r_sum1.wrote(sum_mm[-1])
        r_sq1.wrote(sq_mm[-1])
        f1 = p.op("V", lambda v: v.tensor_scalar_mul(lnb_m[:, :], ps_sum[:, :], 1.0 / D),
                  deps=[sum_mm[-1]] + r_lnb.write_deps())
        r_sum1.read(f1)
        f2 = p.op("V", lambda v: v.tensor_scalar(lnb_a[:, :], ps_sq[:, :], 1.0 / D, EPS,
                                                 ALU.mult, ALU.add),
                  deps=[sq_mm[-1]] + r_lnb.write_deps())
        r_sq1.read(f2)
        f4 = p.op("V", lambda v: v.scalar_tensor_tensor(lnb_b[:, :], lnb_m[:, :], 0.0,
                                                        lnb_a[:, :], ALU.mult, ALU.add),
                  deps=[f1, f2]) if False else None
        f4 = p.op("V", lambda v: v.tensor_tensor(lnb_b[:, :], lnb_m[:, :], lnb_m[:, :],
                                                 ALU.mult), deps=[f1, f2])
        f5 = p.op("V", lambda v: v.tensor_sub(lnb_a[:, :], lnb_a[:, :], lnb_b[:, :]),
                  deps=[f4])
        f6 = p.op("V", lambda v: v.reciprocal(lnb_b[:, :], lnb_a[:, :]), deps=[f5])
        f7 = p.op("S", lambda s: s.activation(lnb_a[:, :], lnb_b[:, :], AF.Sqrt), deps=[f6])
        last = None
        for di in range(NC):
            sl = di % 2
            u1 = p.op("V", lambda v, di=di, sl=sl, gf=gf: v.tensor_sub(
                tmpbuf[:, ts(sl, 512)], gf(di), lnb_m[:, :]),
                deps=[f1, ldx, f7] + tmp_slot1[sl].write_deps())
            tmp_slot1[sl].wrote(u1)
            xslot[par].read(u1)
            u2 = p.op("V", lambda v, di=di, sl=sl, tc=tc: v.tensor_tensor(
                xn[:, di * S + tc * 512: di * S + (tc + 1) * 512],
                tmpbuf[:, ts(sl, 512)], lnb_a[:, :], ALU.mult),
                deps=[u1])
            tmp_slot1[sl].read(u2)
            last = u2
        r_lnb.wrote(last)
        ln1_chunks.append(last)
        if tc + 2 <= 3:
            get_ldx(tc + 2)

    ln1_psum_last = (r_sum1.readers[-1:] + r_sq1.readers[-1:])
    lnp_scope.close()
    ln_scope.close()

    d_xn = dbg_t("xn", [128, NC * S], BF16)
    dump("SP", d_xn, xn[:, :], ln1_chunks)

    if stage == "ln1":
        xn_scope.close()
        oall_scope.close()
        est.close()
        return _finish(nc, p, dbg_ops, None), dbg

    att_scope = ExitStack()
    asb = lambda name, shape, dtype: att_scope.enter_context(nc.sbuf_tensor(name, shape, dtype))
    wqb = [asb(f"wqb{i}", [128, D], BF16) for i in range(2)]
    wkb = [asb(f"wkb{i}", [128, D], BF16) for i in range(2)]
    wvb = [asb(f"wvb{i}", [128, D], BF16) for i in range(2)]
    k_h = [asb(f"k_h{i}", [128, S], BF16) for i in range(2)]
    v_f = [asb(f"v_f{i}", [128, S], BF16) for i in range(2)]
    v_t = [asb(f"v_t{i}", [128, S], BF16) for i in range(2)]
    q_h = [asb(f"q_h{i}", [128, NOWN], BF16) for i in range(2)]
    pt_b = asb("pt", [128, NT * NOWN], BF16)
    sqb = asb("sqb", [128, NOWN], BF16)
    nrm = asb("nrm", [1, NOWN + 16], F32)
    mtmp = asb("mtmp", [1, NOWN], F32)
    mrow = [asb(f"mrow{i}", [1, NOWN], BF16) for i in range(2)]
    invb = asb("invb", [128, NOWN], F32)

    attp_scope = ExitStack()
    pR = attp_scope.enter_context(nc.psum_tensor("pR", [128, NOWN], F32))
    pP = [attp_scope.enter_context(nc.psum_tensor(f"pP{i}", [128, 512], F32)) for i in range(4)]
    rp = [Res(ln1_psum_last) for _ in range(4)]
    rR = Res(ln1_psum_last)
    rB67 = [Res(ln1_psum_last), Res(ln1_psum_last)]

    wslot = {k: [Res(), Res()] for k in ("wq", "wk", "wv")}
    rk_s = [Res(), Res()]
    rv_s = [Res(), Res()]
    rvt_s = [Res(), Res()]
    rq_s = [Res(), Res()]
    r_sqb = Res()
    r_mtmp = Res()
    r_mrow_s = [Res(), Res()]
    r_invb = Res()
    r_pt = [Res() for _ in range(NT)]
    r_oall = Res()

    wbuf = {"wq": wqb, "wk": wkb, "wv": wvb}

    def load_w(kind, src, h):
        par = h % 2
        slot = wslot[kind][par]
        o = p.dma("G", lambda q, src=src, h=h, par=par, kind=kind: q.dma_start(
            wbuf[kind][par][:, :], src[h]),
            deps=slot.write_deps() + (boot_all if h == 0 else []), sem_key=f"{kind}{par}")
        slot.wrote(o)
        return o

    wload = {}
    for h0 in range(min(2, n_heads)):
        for kind, src in (("wq", wq4), ("wk", wk4), ("wv", wv4)):
            wload[(kind, h0)] = load_w(kind, src, h0)

    qstart = [128 * (kc // 2) for kc in range(NT)]
    head_out_ops = []
    d_k0 = dbg_t("k0", [128, S], BF16)
    d_vt0 = dbg_t("vt0", [128, S], BF16)
    d_q0 = dbg_t("q0", [128, NOWN], BF16)
    d_mrow0 = dbg_t("mrow0", [1, NOWN], BF16)
    d_nrm0 = dbg_t("nrm0", [1, NOWN + 16], F32)
    d_pt0 = dbg_t("pt0", [128, NT * NOWN], BF16)

    def emit_proj_bounds(h):
        par = h % 2
        kb, vb, qb = k_h[par], v_f[par], q_h[par]
        rk, rv, rq = rk_s[par], rv_s[par], rq_s[par]
        r_mrow = r_mrow_s[par]

        k_ev = []
        k_mm_last = None
        for tc in range(4):
            bank = tc % 2
            mms = []
            for di in range(NC):
                mm = p.op("T", lambda t, di=di, tc=tc, bank=bank, par=par: t.matmul(
                    pP[bank][:, :], wkb[par][:, ts(di)],
                    xn[:, di * S + tc * 512: di * S + (tc + 1) * 512],
                    start=(di == 0), stop=(di == NC - 1)),
                    deps=([wload[("wk", h)], ln1_chunks[tc]] + rp[bank].write_deps()
                          + (rk.write_deps() if tc < 2 else [])) if di == 0 else [])
                mms.append(mm)
            rp[bank].wrote(mms[-1])
            k_mm_last = mms[-1]
            ev = p.op("S", lambda s, tc=tc, bank=bank, kb=kb, h=h: s.activation(
                kb[:, tc * 512:(tc + 1) * 512], pP[bank][:, :], AF.Identity,
                bias=bk_sb[:, h:h + 1]),
                deps=[mms[-1]] + (bias_ld if h == 0 and tc == 0 else []))
            rp[bank].read(ev)
            k_ev.append(ev)
        rk.wrote(k_ev[-1])
        wslot["wk"][par].read(k_mm_last)

        v_ev = []
        v_mm_last = None
        for tc in range(4):
            bank = 2 + tc % 2
            mms = []
            for di in range(NC):
                mm = p.op("T", lambda t, di=di, tc=tc, bank=bank, par=par: t.matmul(
                    pP[bank][:, :], wvb[par][:, ts(di)],
                    xn[:, di * S + tc * 512: di * S + (tc + 1) * 512],
                    start=(di == 0), stop=(di == NC - 1)),
                    deps=([wload[("wv", h)], ln1_chunks[tc]] + rp[bank].write_deps()
                          + (rv.write_deps() if tc < 2 else [])) if di == 0 else [])
                mms.append(mm)
            rp[bank].wrote(mms[-1])
            v_mm_last = mms[-1]
            ev = p.op("S", lambda s, tc=tc, bank=bank, vb=vb, h=h: s.activation(
                vb[:, tc * 512:(tc + 1) * 512], pP[bank][:, :], AF.Identity,
                bias=bv_sb[:, h:h + 1]),
                deps=[mms[-1]] + (bias_ld if h == 0 and tc == 0 else []))
            rp[bank].read(ev)
            v_ev.append(ev)
        rv.wrote(v_ev[-1])
        wslot["wv"][par].read(v_mm_last)

        q_ev = []
        q_mm_last = None
        for half in range(2):
            bank = half
            mms = []
            for di in range(NC):
                rhs = bass.AP(xn, di * S + 128 + half * 1024, [[NC * S, 128], [256, 4], [1, 128]])
                mm = p.op("T", lambda t, di=di, bank=bank, rhs=rhs, par=par: t.matmul(
                    pP[bank][:, :], wqb[par][:, ts(di)], rhs,
                    start=(di == 0), stop=(di == NC - 1)),
                    deps=([wload[("wq", h)], ln1_chunks[3]] + rp[bank].write_deps()
                          + (rq.write_deps() if half == 0 else [])) if di == 0 else [])
                mms.append(mm)
            rp[bank].wrote(mms[-1])
            q_mm_last = mms[-1]
            ev = p.op("S", lambda s, half=half, bank=bank, qb=qb, h=h: s.activation(
                qb[:, half * 512:(half + 1) * 512], pP[bank][:, :], AF.Identity,
                bias=bq_sb[:, h:h + 1]),
                deps=[mms[-1]] + (bias_ld if h == 0 and half == 0 else []))
            rp[bank].read(ev)
            q_ev.append(ev)
        rq.wrote(q_ev[-1])
        wslot["wq"][par].read(q_mm_last)

        if h + 2 < n_heads:
            for kind, src in (("wq", wq4), ("wk", wk4), ("wv", wv4)):
                wload[(kind, h + 2)] = load_w(kind, src, h + 2)

        kn_acc = []
        ksq_mm = []
        for tc in range(4):
            sl = tc % 2
            ksq = p.op("V", lambda v, kb=kb, tc=tc, sl=sl: v.tensor_tensor(
                sqb[:, ts(sl, 512)], kb[:, tc * 512:(tc + 1) * 512],
                kb[:, tc * 512:(tc + 1) * 512], ALU.mult),
                deps=[k_ev[tc]] + ([ksq_mm[tc - 2]] if tc >= 2 else r_sqb.write_deps()))
            rk.read(ksq)
            bank = 2 + tc % 2
            mm = p.op("T", lambda t, sl=sl, bank=bank: t.matmul(
                pP[bank][:, :], ones_sq[:], sqb[:, ts(sl, 512)],
                start=True, stop=True),
                deps=[ksq] + rp[bank].write_deps())
            rp[bank].wrote(mm)
            ksq_mm.append(mm)
            if tc == 0:
                ev = p.op("V", lambda v, bank=bank: v.tensor_copy(
                    mtmp[:, 0:512], pP[bank][0:1, :]),
                    deps=[mm] + r_mtmp.write_deps())
            else:
                ev = p.op("V", lambda v, bank=bank: v.tensor_tensor(
                    mtmp[:, 0:512], mtmp[:, 0:512], pP[bank][0:1, :], ALU.add),
                    deps=[mm, kn_acc[-1]])
            rp[bank].read(ev)
            kn_acc.append(ev)
        r_mtmp.wrote(kn_acc[-1])
        ksum = p.op("V", lambda v: v.reduce_sum(nrm[:, NOWN + 9:NOWN + 10],
                                                mtmp[:, 0:512], axis=AX.X), deps=[kn_acc[-1]])
        r_mtmp.read(ksum)
        qsq = p.op("V", lambda v, qb=qb: v.tensor_tensor(sqb[:, 0:NOWN], qb[:, :], qb[:, :],
                                                         ALU.mult),
                   deps=[q_ev[-1], ksq_mm[-1], ksq_mm[-2]])
        rq.read(qsq)
        r_sqb.wrote(qsq)
        qn_ev = []
        for half in range(2):
            bank = half
            mm = p.op("T", lambda t, half=half, bank=bank: t.matmul(
                pP[bank][:, :], ones_sq[:], sqb[:, half * 512:(half + 1) * 512],
                start=True, stop=True),
                deps=[qsq] + rp[bank].write_deps())
            rp[bank].wrote(mm)
            r_sqb.read(mm)
            ev = p.op("V", lambda v, half=half, bank=bank: v.tensor_copy(
                nrm[:, half * 512:(half + 1) * 512], pP[bank][0:1, :]),
                deps=[mm])
            rp[bank].read(ev)
            qn_ev.append(ev)
        t1 = p.op("S", lambda s: s.activation(mtmp[:, :], nrm[:, 0:NOWN], AF.Ln,
                                              scale=nrm[:, NOWN + 9:NOWN + 10]),
                  deps=qn_ev + [ksum] + r_mtmp.write_deps())
        r_mtmp.wrote(t1)
        t2 = p.op("S", lambda s: s.activation(nrm[:, 0:NOWN], mtmp[:, :], AF.Exp,
                                              scale=0.5, bias=lnc[:, 0:1]), deps=[t1])
        r_mtmp.read(t2)
        mr = p.op("V", lambda v, par=par: v.tensor_scalar_mul(mrow[par][:, :], nrm[:, 0:NOWN], -1.0),
                  deps=[t2] + r_mrow.write_deps())
        r_mrow.wrote(mr)
        return dict(k_ev=k_ev, v_ev=v_ev, q_ev=q_ev, mr=mr)

    def emit_attn_core(h, pb):
        par = h % 2
        kb, vb, vt, qb = k_h[par], v_f[par], v_t[par], q_h[par]
        rk, rv, rvt, rq = rk_s[par], rv_s[par], rvt_s[par], rq_s[par]
        r_mrow = r_mrow_s[par]
        k_ev, v_ev, q_ev, mr = pb["k_ev"], pb["v_ev"], pb["q_ev"], pb["mr"]

        with nc.psum_tensor(f"pT0_{h}", [128, 1024], BF16) as pT0, \
             nc.psum_tensor(f"pT1_{h}", [128, 1024], BF16) as pT1:
            pTs = [pT0, pT1]
            tr_ev = []
            for kc in range(NT):
                bk_ = kc % 2
                trm = p.op("T", lambda t, kc=kc, bk_=bk_, vb=vb, pTs=pTs: t.transpose(
                    pTs[bk_][:, 0:128], vb[:, ts(kc)], ident[:]),
                    deps=[v_ev[kc // 4]] + rB67[bk_].write_deps())
                rB67[bk_].wrote(trm)
                rv.read(trm)
                ev = p.op("V", lambda v, kc=kc, bk_=bk_, vt=vt, pTs=pTs: v.tensor_copy(
                    vt[:, ts(kc)], pTs[bk_][:, 0:128]),
                    deps=[trm] + (rvt.write_deps() if kc < 2 else []))
                rB67[bk_].read(ev)
                tr_ev.append(ev)
            rvt.wrote(tr_ev[-1])

        with nc.psum_tensor(f"pO0_{h}", [128, 512], F32) as pO0, \
             nc.psum_tensor(f"pO1_{h}", [128, 512], F32) as pO1:
            pOs = [pO0, pO1]
            exp_done = {}
            pv_ops = []
            r_ops = []

            def emit_st(kc, kb=kb, qb=qb, par=par):
                qs = qstart[kc]
                pair = (0, 1) if (kc % 2 == 0) else (2, 3)
                segs = []
                for half in range(2):
                    lo = max(qs, half * 512)
                    hi = (half + 1) * 512
                    if lo >= hi:
                        continue
                    w_ = hi - lo
                    bank = pair[half]
                    mm1 = p.op("T", lambda t, kc=kc, lo=lo, w_=w_, bank=bank: t.matmul(
                        pP[bank][:, 0:w_], kb[:, ts(kc)], qb[:, lo:lo + w_],
                        start=True, stop=False),
                        deps=[k_ev[kc // 4], q_ev[lo // 512]] + rp[bank].write_deps())
                    mm2 = p.op("T", lambda t, kc=kc, lo=lo, w_=w_, bank=bank: t.matmul(
                        pP[bank][:, 0:w_], ones_r[:], mrow[par][:, lo:lo + w_],
                        start=False, stop=True), deps=[mr])
                    rp[bank].wrote(mm2)
                    rk.read(mm1)
                    rq.read(mm1)
                    r_mrow.read(mm2)
                    bias_ap = pmb_sb[:, 0:1] if kc == 0 else zcol[:, 0:1]
                    evx = p.op("S", lambda s, kc=kc, lo=lo, w_=w_, bank=bank, bias_ap=bias_ap: s.activation(
                        pt_b[:, kc * NOWN + lo: kc * NOWN + lo + w_], pP[bank][:, 0:w_],
                        AF.Exp, bias=bias_ap, scale=S2),
                        deps=[mm2] + (r_pt[kc].write_deps() if lo == qs else []))
                    rp[bank].read(evx)
                    segs.append(evx)
                last = segs[-1]
                if kc % 2 == 1:
                    i0 = (kc - 1) // 2
                    dm = p.op("V", lambda v, kc=kc, i0=i0: v.tensor_tensor(
                        pt_b[:, kc * NOWN + i0 * 128: kc * NOWN + (i0 + 1) * 128],
                        pt_b[:, kc * NOWN + i0 * 128: kc * NOWN + (i0 + 1) * 128],
                        tri[:], ALU.mult), deps=segs)
                    last = dm
                r_pt[kc].wrote(last)
                exp_done[kc] = last

            def emit_pv(kc, vt=vt):
                qs = qstart[kc]
                last = exp_done[kc]
                for half in range(2):
                    lo = max(qs, half * 512)
                    hi = (half + 1) * 512
                    if lo >= hi:
                        continue
                    w_ = hi - lo
                    mmo = p.op("T", lambda t, kc=kc, lo=lo, w_=w_, half=half, pOs=pOs: t.matmul(
                        pOs[half][:, lo - half * 512: lo - half * 512 + w_],
                        vt[:, ts(kc)], pt_b[:, kc * NOWN + lo: kc * NOWN + lo + w_],
                        start=(kc == 0), stop=(kc == NT - 1)),
                        deps=[last, tr_ev[kc]] + (rB67[half].write_deps() if kc == 0 else []))
                    if kc == 0:
                        rB67[half].wrote(mmo)
                    else:
                        rB67[half].touch(mmo)
                    rvt.read(mmo)
                    r_pt[kc].read(mmo)
                    pv_ops.append(mmo)
                    mmr = p.op("T", lambda t, kc=kc, lo=lo, w_=w_: t.matmul(
                        pR[:, lo:lo + w_], ones_sq[:], pt_b[:, kc * NOWN + lo: kc * NOWN + lo + w_],
                        start=(kc == 0), stop=(kc == NT - 1)),
                        deps=[last] + (rR.write_deps() if kc == 0 else []))
                    if kc == 0:
                        rR.wrote(mmr)
                    else:
                        rR.touch(mmr)
                    r_pt[kc].read(mmr)
                    r_ops.append(mmr)

            emit_st(0)
            emit_st(1)
            for kc in range(2, NT):
                emit_st(kc)
                emit_pv(kc - 2)
            emit_pv(NT - 2)
            emit_pv(NT - 1)

            o_evs = []
            rcps = []
            for half in range(2):
                rcp = p.op("V", lambda v, half=half: v.reciprocal(
                    invb[:, half * 512:(half + 1) * 512], pR[:, half * 512:(half + 1) * 512]),
                    deps=[r_ops[-1]] + (r_invb.write_deps() if half == 0 else []))
                rR.read(rcp)
                rcps.append(rcp)
                oe = p.op("V", lambda v, half=half, h=h, pOs=pOs: v.tensor_tensor(
                    o_all[:, h * NOWN + half * 512: h * NOWN + (half + 1) * 512],
                    pOs[half][:, :], invb[:, half * 512:(half + 1) * 512], ALU.mult),
                    deps=[rcp, pv_ops[-1], pv_ops[-2] if len(pv_ops) >= 2 else None])
                rB67[half].read(oe)
                r_invb.read(oe)
                o_evs.append(oe)
            r_invb.writer = o_evs[-1]
            head_out_ops.append(o_evs[-1])

        if h == 0:
            dump("SP", d_k0, k_h[0][:, :], [k_ev[-1]])
            rk.read(dump("SP", d_vt0, v_t[0][:, :], [tr_ev[-1]]))
            dump("SP", d_q0, q_h[0][:, :], [q_ev[-1]])
            dump("SP", d_mrow0, mrow[0][:, :], [mr])
            dump("SP", d_nrm0, nrm[:, :], [mr])
            if d_pt0 is not None:
                o = dump("SP", d_pt0, pt_b[:, :], [exp_done[kc] for kc in range(NT)])
                for kc in range(NT):
                    r_pt[kc].read(o)

    pb_next = emit_proj_bounds(0)
    for h in range(n_heads):
        pb_cur = pb_next
        if h + 1 < n_heads:
            pb_next = emit_proj_bounds(h + 1)
        emit_attn_core(h, pb_cur)

    att_last = head_out_ops[-1]
    d_oall = dbg_t("o_all", [128, H * NOWN], BF16)
    do_ = dump("SP", d_oall, o_all[:, :], [att_last])
    r_oall.read(do_)

    attp_scope.close()
    att_scope.close()
    xn_scope.close()

    if stage == "attn":
        oall_scope.close()
        est.close()
        return _finish(nc, p, dbg_ops, None), dbg

    x1 = sb("x1", [128, NC * NOWN], F32, side="right")
    wo_scope = ExitStack()
    wob = [wo_scope.enter_context(nc.sbuf_tensor(f"wob{i}", [128, D], BF16)) for i in range(2)]
    x_own = wo_scope.enter_context(nc.sbuf_tensor("x_own", [128, NC * NOWN], F32))

    wop_scope = ExitStack()
    pW = [wop_scope.enter_context(nc.psum_tensor(f"pW{i}", [128, 512], F32)) for i in range(4)]
    rW = [Res([att_last]) for _ in range(4)]

    xo_guard = [att_last] + (dbg_ops[-1:] if dbg_ops else [])
    xo_l = []
    for di in range(NC):
        in_ap = bass.AP(xt, di * 128 * S + 128, [[S, 128], [256, 8], [1, 128]])
        xo_l.append(p.dma("SP", lambda q, di=di, b=in_ap: q.dma_start(
            x_own[:, di * NOWN:(di + 1) * NOWN], b), deps=xo_guard, sem_key=f"xo{di % 2}"))

    wo_slot = [Res(xo_guard), Res(xo_guard)]

    def load_wo(m):
        par = m % 2
        o = p.dma("G", lambda q, m=m, par=par: q.dma_start(wob[par][:, :], wo4[m]),
                  deps=wo_slot[par].write_deps(), sem_key=f"wo{par}")
        wo_slot[par].wrote(o)
        return o

    wo_l = {0: load_wo(0), 1: load_wo(1)}
    x1_ready = {}
    for m in range(NC):
        par = m % 2
        banks = (0, 1) if par == 0 else (2, 3)
        last_mm_b = None
        for half in range(2):
            bank = banks[half]
            mms = []
            for di in range(NC):
                mm = p.op("T", lambda t, m=m, di=di, half=half, bank=bank, par=par: t.matmul(
                    pW[bank][:, :], wob[par][:, ts(di)],
                    o_all[:, di * NOWN + half * 512: di * NOWN + (half + 1) * 512],
                    start=(di == 0), stop=False),
                    deps=([wo_l[m], att_last] + rW[bank].write_deps()) if di == 0 else [])
                mms.append(mm)
            if m < 2 and half == 0:
                r_oall.read(mms[0])
            mmb = p.op("T", lambda t, m=m, bank=bank: t.matmul(
                pW[bank][:, :], bo_sb[:, ts(m)], ones_row[:, :], start=False, stop=True),
                deps=[bias_ld[-1]] if m == 0 else [])
            rW[bank].wrote(mmb)
            wo_slot[par].read(mmb)
            last_mm_b = mmb
            ev = p.op("V", lambda v, m=m, half=half, bank=bank: v.tensor_tensor(
                x1[:, m * NOWN + half * 512: m * NOWN + (half + 1) * 512],
                pW[bank][:, :], x_own[:, m * NOWN + half * 512: m * NOWN + (half + 1) * 512],
                ALU.add), deps=[mmb, xo_l[m]])
            rW[bank].read(ev)
            x1_ready[(m, half)] = ev
        if m + 2 < NC:
            wo_l[m + 2] = load_wo(m + 2)

    x1_all = [x1_ready[(m, hf)] for m in range(NC) for hf in range(2)]
    d_x1 = dbg_t("x1", [128, NC * NOWN], F32)
    dump("SP", d_x1, x1[:, :], x1_all)

    wop_scope.close()
    wo_scope.close()
    oall_scope.close()
    if stage == "wo":
        est.close()
        return _finish(nc, p, dbg_ops, None), dbg

    xn2 = sb("xn2", [128, NC * NOWN], FP8)
    act_b = sb("act", [128, NH * NOWN], FP8)
    ffn_scope = ExitStack()
    fsb = lambda name, shape, dtype: ffn_scope.enter_context(nc.sbuf_tensor(name, shape, dtype))
    wgb = [fsb(f"wgb{i}", [128, 2 * D], FP8) for i in range(2)]
    wlb = [fsb(f"wlb{i}", [128, 4 * D], FP8) for i in range(2)]
    _wl8b1 = fsb("wl8b", [128, D], FP8)
    wl8buf = [_wl8b1, _wl8b1]
    tgb = fsb("tgb", [128, 2 * 512], BF16)
    ostg = fsb("ostg", [128, 16 * 256], F32)
    l2sq = fsb("l2sq", [128, 2 * 512], BF16)
    l2bf = fsb("l2bf", [128, 2 * 512], BF16)
    l2b_m = fsb("l2b_m", [128, 512], F32)
    l2b_a = fsb("l2b_a", [128, 512], F32)
    l2tmp = fsb("l2tmp", [128, 2 * 512], F32)
    l2b_b = l2tmp[:, 0:512]

    ln2p_scope = ExitStack()
    p2_sum = ln2p_scope.enter_context(nc.psum_tensor("p2_sum", [128, 512], F32))
    p2_sq = ln2p_scope.enter_context(nc.psum_tensor("p2_sq", [128, 512], F32))
    wo_psum_last = [rW[i].readers[-1] for i in range(4) if rW[i].readers]
    r_sum2, r_sq2 = (Res(wo_psum_last) for _ in range(2))
    r_lnb2 = Res()
    sq_slot2 = [Res(), Res()]
    tmp_slot2 = [Res(), Res()]
    l2bf_slot = [Res(), Res()]

    xn2_done = []
    for tc in range(2):
        sum_mm = []
        sq_mm = []
        for di in range(NC):
            sl = di % 2
            src = l2bf[:, ts(sl, 512)]
            cp = p.op("S", lambda s, di=di, sl=sl, tc=tc: s.activation(
                l2bf[:, ts(sl, 512)],
                x1[:, di * NOWN + tc * 512: di * NOWN + (tc + 1) * 512], AF.Identity),
                deps=[x1_ready[(di, tc)]] + l2bf_slot[sl].write_deps())
            l2bf_slot[sl].wrote(cp)
            sq_op = p.op("V", lambda v, sl=sl, src=src: v.tensor_tensor(
                l2sq[:, ts(sl, 512)], src, src, ALU.mult),
                deps=[cp] + sq_slot2[sl].write_deps())
            sq_slot2[sl].wrote(sq_op)
            l2bf_slot[sl].read(sq_op)
            m1 = p.op("T", lambda t, di=di, src=src: t.matmul(
                p2_sum[:, :], ones_sq[:], src, start=(di == 0), stop=(di == NC - 1)),
                deps=([cp] + r_sum2.write_deps()) if di == 0 else [cp])
            m2 = p.op("T", lambda t, di=di, sl=sl: t.matmul(
                p2_sq[:, :], ones_sq[:], l2sq[:, ts(sl, 512)],
                start=(di == 0), stop=(di == NC - 1)),
                deps=[sq_op] + (r_sq2.write_deps() if di == 0 else []))
            sq_slot2[sl].read(m2)
            l2bf_slot[sl].read(m1)
            sum_mm.append(m1)
            sq_mm.append(m2)
        r_sum2.wrote(sum_mm[-1])
        r_sq2.wrote(sq_mm[-1])
        f1 = p.op("V", lambda v: v.tensor_scalar_mul(l2b_m[:, :], p2_sum[:, :], 1.0 / D),
                  deps=[sum_mm[-1]] + r_lnb2.write_deps())
        r_sum2.read(f1)
        f2 = p.op("V", lambda v: v.tensor_scalar(l2b_a[:, :], p2_sq[:, :], 1.0 / D, EPS,
                                                 ALU.mult, ALU.add),
                  deps=[sq_mm[-1]] + r_lnb2.write_deps())
        r_sq2.read(f2)
        f4 = p.op("V", lambda v: v.tensor_tensor(l2b_b, l2b_m[:, :], l2b_m[:, :],
                                                 ALU.mult), deps=[f1, f2])
        f5 = p.op("V", lambda v: v.tensor_sub(l2b_a[:, :], l2b_a[:, :], l2b_b),
                  deps=[f4])
        f6 = p.op("V", lambda v: v.reciprocal(l2b_b, l2b_a[:, :]), deps=[f5])
        f7 = p.op("S", lambda s: s.activation(l2b_a[:, :], l2b_b, AF.Sqrt), deps=[f6])
        last = None
        for di in range(NC):
            sl = di % 2
            u1 = p.op("V", lambda v, di=di, sl=sl, tc=tc: v.tensor_sub(
                l2tmp[:, ts(sl, 512)],
                x1[:, di * NOWN + tc * 512: di * NOWN + (tc + 1) * 512], l2b_m[:, :]),
                deps=[f1, f7] + tmp_slot2[sl].write_deps())
            tmp_slot2[sl].wrote(u1)
            u2 = p.op("V", lambda v, di=di, sl=sl, tc=tc: v.tensor_tensor(
                xn2[:, di * NOWN + tc * 512: di * NOWN + (tc + 1) * 512],
                l2tmp[:, ts(sl, 512)], l2b_a[:, :], ALU.mult),
                deps=[u1])
            tmp_slot2[sl].read(u2)
            last = u2
        r_lnb2.wrote(last)
        xn2_done.append(last)
    ln2_psum_last = (r_sum2.readers[-1:] + r_sq2.readers[-1:] + [xn2_done[-1]])
    ln2p_scope.close()
    d_xn2 = dbg_t("xn2", [128, NC * NOWN], F32)
    dump("G", d_xn2, xn2[:, :], xn2_done)

    mm1p_scope = ExitStack()
    pG = [mm1p_scope.enter_context(nc.psum_tensor(f"pG{i}", [128, 512], F32)) for i in range(2)]
    pF = [mm1p_scope.enter_context(nc.psum_tensor(f"pF{i}", [128, 512], F32)) for i in range(2)]
    rG = [Res(ln2_psum_last), Res(ln2_psum_last)]
    rF = [Res(ln2_psum_last), Res(ln2_psum_last)]

    ffn_guard = [x1_all[-1]] + (dbg_ops[-1:] if dbg_ops else [])
    wg_slot = [Res(ffn_guard), Res(ffn_guard)]
    _wl8slot1 = Res(ffn_guard)
    wl8_slot = [_wl8slot1, _wl8slot1]
    wl8_done = [None] * NH

    def load_wg(m):
        par = m % 2
        wd = wg_slot[par].write_deps()
        d1 = p.dma("G", lambda q, m=m, par=par: q.dma_start(wgb[par][:, 0:D], wg0[m]),
                   deps=wd, sem_key=f"wg{par}a")
        d2 = p.dma("G", lambda q, m=m, par=par: q.dma_start(wgb[par][:, D:2 * D], wg1[m]),
                   deps=wd, sem_key=f"wg{par}b")
        wg_slot[par].wrote(d2)
        return (d1, d2)

    wg_l = {0: load_wg(0), 1: load_wg(1)}
    act_ready = [None] * NH
    last_ge = None

    for m in range(NH):
        par = m % 2
        d1, d2 = wg_l[m]
        for half in range(2):
            mg = []
            for di in range(NC):
                mm = p.op("T", lambda t, di=di, half=half, par=par: t.matmul(
                    pG[par][:, :], wgb[par][:, ts(di)],
                    xn2[:, di * NOWN + half * 512: di * NOWN + (half + 1) * 512],
                    start=(di == 0), stop=(di == NC - 1)),
                    deps=([d1, xn2_done[-1]] + rG[par].write_deps()) if di == 0 else [])
                mg.append(mm)
            rG[par].wrote(mg[-1])
            mf = []
            for di in range(NC):
                mm = p.op("T", lambda t, di=di, half=half, par=par: t.matmul(
                    pF[par][:, :], wgb[par][:, D + di * 128:D + (di + 1) * 128],
                    xn2[:, di * NOWN + half * 512: di * NOWN + (half + 1) * 512],
                    start=(di == 0), stop=(di == NC - 1)),
                    deps=([d2] + rF[par].write_deps()) if di == 0 else [])
                mf.append(mm)
            rF[par].wrote(mf[-1])
            ge = p.op("S", lambda s, half=half, par=par: s.activation(
                tgb[:, ts(par, 512)], pG[par][:, :], AF.Gelu_apprx_tanh), deps=[mg[-1]])
            rG[par].read(ge)
            last_ge = ge
            mu = p.op("V", lambda v, m=m, half=half, par=par: v.tensor_tensor(
                act_b[:, m * NOWN + half * 512: m * NOWN + (half + 1) * 512],
                tgb[:, ts(par, 512)], pF[par][:, :], ALU.mult), deps=[ge, mf[-1]])
            rF[par].read(mu)
            wg_slot[par].read(mu)
            act_ready[m] = mu
        if m + 2 < NH:
            wg_l[m + 2] = load_wg(m + 2)
        hh = m
        pslot = wl8_slot[hh % 2]
        ld8 = p.dma("G", lambda q, hh=hh: q.dma_start(wl8buf[hh % 2][:, :], wl[hh]),
                    deps=pslot.write_deps(), sem_key="wl8l")
        st8 = p.dma("SP", lambda q, hh=hh: q.dma_start(wl8[hh], wl8buf[hh % 2][:, :]),
                    deps=[ld8], sem_key="wl8s")
        pslot.wrote(st8)
        wl8_done[hh] = st8

    d_act = dbg_t("act", [128, NH * NOWN], F32)
    dump("G", d_act, act_b[:, :], [act_ready[NH - 1]])
    mm1_psum_last = [op for op in (last_ge, act_ready[NH - 1]) if op is not None]
    mm1p_scope.close()

    mm2p_scope = ExitStack()
    pM = mm2p_scope.enter_context(nc.psum_tensor("pM", [128, NC * 256], F32))
    rM = Res(mm1_psum_last)

    wl_slot = [Res(), Res()]
    r_ostg = [Res(), Res()]
    out_sts = []

    def load_wl8(qt, blk):
        par = blk % 2
        in_ap = bass.AP(wl8, blk * 4 * 128 * D, [[D, 128], [128 * D, 4], [1, D]])
        o = p.dma("G", lambda q, b=in_ap, par=par: q.dma_start(wlb[par][:, :], b),
                  deps=wl_slot[par].write_deps() + [wl8_done[(blk + 1) * 4 - 1]],
                  sem_key=f"wlq{par}")
        wl_slot[par].wrote(o)
        return o

    pending = {(0, 0): load_wl8(0, 0), (0, 1): load_wl8(0, 1)}
    NBLK = 16
    for qt in range(4):
        quarter_last_mm = None
        for blk in range(NBLK):
            par = blk % 2
            ld = pending.pop((qt, blk))
            blk_last = None
            for hi in range(4):
                hh = blk * 4 + hi
                for f in range(NC):
                    st_flag = (hh == 0 and f % 2 == 0)
                    mm = p.op("T", lambda t, qt=qt, hi=hi, hh=hh, f=f, par=par, st_flag=st_flag: t.matmul(
                        pM[:, f * 256:(f + 1) * 256],
                        wlb[par][:, hi * D + f * 128: hi * D + (f + 1) * 128],
                        act_b[:, hh * NOWN + qt * 256: hh * NOWN + qt * 256 + 256],
                        start=st_flag, stop=(hh == NH - 1)),
                        deps=([ld, act_ready[hh]] + (rM.write_deps() if hh == 0 else [])) if f == 0 else [])
                    blk_last = mm
            wl_slot[par].read(blk_last)
            nxt = (qt, blk + 2) if blk + 2 < NBLK else ((qt + 1, blk + 2 - NBLK) if qt < 3 else None)
            if nxt is not None and nxt not in pending:
                pending[nxt] = load_wl8(*nxt)
            quarter_last_mm = blk_last
        rM.wrote(quarter_last_mm)
        evs = []
        for fg in range(2):
            for f8 in range(8):
                f = fg * 8 + f8
                ev = p.op("V", lambda v, qt=qt, f=f, f8=f8, fg=fg: v.tensor_tensor(
                    ostg[:, fg * 2048 + f8 * 256: fg * 2048 + (f8 + 1) * 256],
                    pM[:, f * 256:(f + 1) * 256],
                    x1[:, f * NOWN + qt * 256: f * NOWN + qt * 256 + 256], ALU.add),
                    deps=[quarter_last_mm] + (r_ostg[fg].write_deps() if f8 == 0 else []))
                rM.read(ev)
                evs.append(ev)
            out_ap = bass.AP(out, fg * 8 * 128 * NOWN + qt * 256,
                             [[NOWN, 128], [128 * NOWN, 8], [1, 256]])
            st = p.dma("SP", lambda q, a=out_ap, fg=fg: q.dma_start(
                a, ostg[:, fg * 2048:(fg + 1) * 2048]),
                deps=evs[-8:], sem_key="out")
            r_ostg[fg].wrote(evs[-1])
            r_ostg[fg].read(st)
            out_sts.append(st)

    mm2p_scope.close()
    ffn_scope.close()
    est.close()
    return _finish(nc, p, dbg_ops, out_sts), dbg


def _finish(nc, p, dbg_ops, out_sts):
    keys = p.lower()
    with ExitStack() as sctx:
        sems = {k: sctx.enter_context(nc.semaphore(f"s_{k}")) for k in sorted(keys)}
        with nc.Block() as block:
            @block.tensor
            def _(t):
                p.emit_engine("T", t, sems)

            @block.vector
            def _(v):
                p.emit_engine("V", v, sems)

            @block.scalar
            def _(s):
                p.emit_engine("S", s, sems)

            @block.gpsimd
            def _(g):
                p.emit_engine("G", g, sems)

            @block.sync
            def _(sp):
                p.emit_engine("SP", sp, sems)
                if out_sts:
                    sp.wait_ge(sems["out"], p.final_counts["out"])
                if dbg_ops and "dbg" in sems:
                    sp.wait_ge(sems["dbg"], p.final_counts["dbg"])
    return nc



def _prep_inputs(L0_attn_X, L0_attn_Wq, L0_attn_bq, L0_attn_Wk, L0_attn_bk,
                 L0_attn_Wv, L0_attn_bv, L0_attn_Wo, L0_attn_bo,
                 L0_ffn_w_gating, L0_ffn_w_linear):
    X = np.ascontiguousarray(np.asarray(L0_attn_X, dtype=np.float32))
    Wq = np.asarray(L0_attn_Wq, dtype=np.float32)
    Wk = np.asarray(L0_attn_Wk, dtype=np.float32)
    Wv = np.asarray(L0_attn_Wv, dtype=np.float32)
    Wo = np.asarray(L0_attn_Wo, dtype=np.float32)
    wg = np.asarray(L0_ffn_w_gating, dtype=np.float32)
    wlin = np.asarray(L0_ffn_w_linear, dtype=np.float32)

    def head_tiles(W):
        t = W.reshape(H, 128, NC, 128)
        return np.ascontiguousarray(t.transpose(0, 3, 2, 1)).reshape(H, 128, D)

    wq4 = head_tiles(Wq)
    wk4 = head_tiles(Wk)
    wv4 = head_tiles(Wv)
    wo4 = head_tiles(Wo)

    def gate_tiles(Wg):
        t = Wg.reshape(NC, 128, NH, 128)
        return np.ascontiguousarray(t.transpose(2, 1, 0, 3)).reshape(NH, 128, D)

    wg0 = gate_tiles(wg[0])
    wg1 = gate_tiles(wg[1])
    wl_t = np.ascontiguousarray(wlin.reshape(NH, 128, D))

    bq_r = np.ascontiguousarray(np.asarray(L0_attn_bq, np.float32).reshape(H, 128).T)
    bk_r = np.ascontiguousarray(np.asarray(L0_attn_bk, np.float32).reshape(H, 128).T)
    bv_r = np.ascontiguousarray(np.asarray(L0_attn_bv, np.float32).reshape(H, 128).T)
    bo_r = np.ascontiguousarray(np.asarray(L0_attn_bo, np.float32).reshape(1, D))

    in_maps = []
    for c in range(8):
        b = c // 2
        odd = c % 2
        XT = np.ascontiguousarray(X[b].T)
        if odd:
            xt_c = XT
        else:
            xt_c = np.ascontiguousarray(
                np.concatenate([np.zeros((D, 128), np.float32), XT[:, :S - 128]], axis=1))
        pmb_c = np.full((128, 1), (0.0 if odd else -1e30), np.float32)
        in_maps.append({
            "xt": xt_c,
            "wq4": wq4, "wk4": wk4, "wv4": wv4, "wo4": wo4,
            "bq": bq_r, "bk": bk_r, "bv": bv_r, "bo": bo_r,
            "wg0": wg0, "wg1": wg1, "wl": wl_t,
            "pmb": pmb_c,
        })
    return in_maps


def _assemble(results):
    outf = np.empty((B, S, D), dtype=np.float32)
    for c in range(8):
        b = c // 2
        odd = c % 2
        oc = results[c]["out"]
        for i in range(8):
            t_true = 2 * i + odd
            outf[b, t_true * 128:(t_true + 1) * 128, :] = oc[:, i * 128:(i + 1) * 128].T
    return outf


_CACHED = {}


def kernel(**inputs):
    in_maps = _prep_inputs(**inputs)
    if "nc" not in _CACHED:
        _CACHED["nc"], _ = build_program()
    res = run_bass_kernel_spmd(_CACHED["nc"], in_maps, core_ids=list(range(8)))
    return _assemble(res.results)


# version 05
# speedup vs baseline: 1.1901x; 1.1901x over previous
import numpy as np
from contextlib import ExitStack

import concourse.bass as bass
import concourse.mybir as mybir
from concourse.bass_utils import run_bass_kernel_spmd

F32 = mybir.dt.float32
BF16 = mybir.dt.bfloat16
FP8 = mybir.dt.float8e4
AF = mybir.ActivationFunctionType
ALU = mybir.AluOpType
AX = mybir.AxisListType

B, S, D, H = 4, 2048, 2048, 16
HID = 8192
NC = 16
NH = 64
NT = 16
NOWN = 1024
S2 = float(1.0 / np.sqrt(128.0))
EPS = 1e-5

ENGINES = ("T", "V", "S", "SP", "G")


class Op:
    __slots__ = ("eng", "emit", "deps", "inc", "sem_key", "needs_inc", "sem_val")

    def __init__(self, eng, emit, deps, inc, sem_key):
        self.eng = eng
        self.emit = emit
        self.deps = [d for d in deps if d is not None]
        self.inc = inc
        self.sem_key = sem_key
        self.needs_inc = False
        self.sem_val = None


class Prog:
    def __init__(self):
        self.eng_ops = {e: [] for e in ENGINES}
        self.last_dma = {}

    def op(self, eng, emit, deps=(), inc=1, sem_key=None):
        o = Op(eng, emit, deps, inc, sem_key or eng)
        if inc == 16:
            prev = self.last_dma.get(o.sem_key)
            if prev is not None:
                o.deps.append(prev)
            self.last_dma[o.sem_key] = o
        self.eng_ops[eng].append(o)
        return o

    def dma(self, eng, emit, deps=(), sem_key=None):
        assert sem_key is not None
        return self.op(eng, emit, deps, inc=16, sem_key=sem_key)

    def lower(self):
        for e in ENGINES:
            for o in self.eng_ops[e]:
                for d in o.deps:
                    if d.eng != o.eng or d.inc == 16:
                        d.needs_inc = True
        counts = {}
        keys = set()
        for e in ENGINES:
            for o in self.eng_ops[e]:
                if o.inc == 16:
                    o.needs_inc = True
                if o.needs_inc:
                    counts[o.sem_key] = counts.get(o.sem_key, 0) + o.inc
                    o.sem_val = counts[o.sem_key]
                    keys.add(o.sem_key)
        self.final_counts = counts
        return keys

    def emit_engine(self, eng_name, handle, sems):
        wm = {}
        for o in self.eng_ops[eng_name]:
            for d in o.deps:
                if d.eng == o.eng and d.inc != 16:
                    continue
                assert d.sem_val is not None, f"dep missing inc {d.eng}/{d.sem_key}"
                if wm.get(d.sem_key, -1) >= d.sem_val:
                    continue
                handle.wait_ge(sems[d.sem_key], d.sem_val)
                wm[d.sem_key] = d.sem_val
            inst = o.emit(handle)
            if o.needs_inc:
                if inst is None:
                    inst = handle.engine_nop()
                inst.then_inc(sems[o.sem_key], o.inc)


class Res:

    def __init__(self, seed_readers=()):
        self.writer = None
        self.readers = [r for r in seed_readers if r is not None]

    def write_deps(self):
        return list(self.readers) if self.readers else ([self.writer] if self.writer else [])

    def wrote(self, op):
        self.writer = op
        self.readers = []

    def touch(self, op):
        self.writer = op

    def read(self, op):
        if op is not None:
            self.readers.append(op)
        return op


def ts(i, n=128):
    return slice(i * n, (i + 1) * n)


def build_program(n_heads=H, stage="full", dbg_names=()):
    nc = bass.Bass("TRN2", target_bir_lowering=False)

    xt = nc.dram_tensor("xt", [D, S], F32, kind="ExternalInput")
    wq4 = nc.dram_tensor("wq4", [H, 128, D], F32, kind="ExternalInput")
    wk4 = nc.dram_tensor("wk4", [H, 128, D], F32, kind="ExternalInput")
    wv4 = nc.dram_tensor("wv4", [H, 128, D], F32, kind="ExternalInput")
    wo4 = nc.dram_tensor("wo4", [NC, 128, D], F32, kind="ExternalInput")
    bq = nc.dram_tensor("bq", [128, H], F32, kind="ExternalInput")
    bk = nc.dram_tensor("bk", [128, H], F32, kind="ExternalInput")
    bv = nc.dram_tensor("bv", [128, H], F32, kind="ExternalInput")
    bo = nc.dram_tensor("bo", [1, D], F32, kind="ExternalInput")
    wg0 = nc.dram_tensor("wg0", [NH, 128, D], F32, kind="ExternalInput")
    wg1 = nc.dram_tensor("wg1", [NH, 128, D], F32, kind="ExternalInput")
    wl = nc.dram_tensor("wl", [NH, 128, D], F32, kind="ExternalInput")
    pmb = nc.dram_tensor("pmb", [128, 1], F32, kind="ExternalInput")
    out = nc.dram_tensor("out", [D, NOWN], F32, kind="ExternalOutput")
    wl8 = nc.dram_tensor("wl8", [NH, 128, D], FP8)
    wg8 = nc.dram_tensor("wg8", [NH, 2, 128, D], FP8)

    dbg = {}

    def dbg_t(name, shape, dtype):
        if name in dbg_names:
            dbg[name] = nc.dram_tensor("dbg_" + name, shape, dtype, kind="ExternalOutput")
        return dbg.get(name)

    p = Prog()
    est = ExitStack()
    sb = lambda name, shape, dtype, **kw: est.enter_context(nc.sbuf_tensor(name, shape, dtype, **kw))
    dbg_ops = []

    def dump(eng, dst, src_ap, deps):
        if dst is None:
            return None
        o = p.dma(eng, lambda q, a=dst, b=src_ap: q.dma_start(a[:, :], b), deps=deps, sem_key="dbg")
        dbg_ops.append(o)
        return o

    ident = sb("ident", [128, 128], BF16)
    ones_sq = sb("ones_sq", [128, 128], BF16)
    tri = sb("tri", [128, 128], BF16)
    ones_r = sb("ones_r", [1, 128], BF16)
    ones_c = sb("ones_c", [128, 1], BF16)
    ones_row = sb("ones_row", [1, 512], BF16)
    zcol = sb("zcol", [128, 1], F32)
    pmb_sb = sb("pmb_sb", [128, 1], F32)
    bq_sb = sb("bq_sb", [128, H], F32)
    bk_sb = sb("bk_sb", [128, H], F32)
    bv_sb = sb("bv_sb", [128, H], F32)
    bo_sb = sb("bo_sb", [1, D], BF16)

    boot = []
    boot.append(p.op("G", lambda g: g.memset(ident[:], 0.0)))
    boot.append(p.op("G", lambda g: g.affine_select(
        out=ident[:], in_=ident[:], compare_op=ALU.not_equal, fill=1.0,
        base=0, pattern=[[-1, 128]], channel_multiplier=1)))
    boot.append(p.op("G", lambda g: g.memset(tri[:], 1.0)))
    boot.append(p.op("G", lambda g: g.affine_select(
        out=tri[:], in_=tri[:], compare_op=ALU.is_ge, fill=0.0,
        base=0, pattern=[[1, 128]], channel_multiplier=-1)))
    boot.append(p.op("G", lambda g: g.memset(ones_sq[:], 1.0)))
    boot.append(p.op("G", lambda g: g.memset(ones_r[:], 1.0)))
    boot.append(p.op("G", lambda g: g.memset(ones_c[:], 1.0)))
    boot.append(p.op("G", lambda g: g.memset(ones_row[:], 1.0)))
    boot.append(p.op("G", lambda g: g.memset(zcol[:], 0.0)))
    lnc = sb("lnc", [1, 8], F32)
    boot.append(p.op("G", lambda g: g.memset(lnc[:], float(0.5 * np.log(25.0 / (128.0 * S))))))
    bias_ld = []
    bias_ld.append(p.dma("SP", lambda q: q.dma_start(pmb_sb[:], pmb[:, :]), sem_key="bias"))
    bias_ld.append(p.dma("SP", lambda q: q.dma_start(bq_sb[:], bq[:, :]), sem_key="bias"))
    bias_ld.append(p.dma("SP", lambda q: q.dma_start(bk_sb[:], bk[:, :]), sem_key="bias"))
    bias_ld.append(p.dma("SP", lambda q: q.dma_start(bv_sb[:], bv[:, :]), sem_key="bias"))
    bias_ld.append(p.dma("G", lambda q: q.dma_start(bo_sb[:], bo[:, :]), sem_key="biasg"))
    boot_all = boot + bias_ld

    oall_scope = ExitStack()
    o_all = oall_scope.enter_context(nc.sbuf_tensor("o_all", [128, H * NOWN], BF16))
    xn_scope = ExitStack()
    xn = xn_scope.enter_context(nc.sbuf_tensor("xn", [128, NC * S], BF16))

    LW = 256
    NLC = S // LW
    ln_scope = ExitStack()
    xf32 = [ln_scope.enter_context(nc.sbuf_tensor(f"xf32_{i}", [128, NC * LW], F32)) for i in range(2)]
    xbf = [ln_scope.enter_context(nc.sbuf_tensor(f"xbf_{i}", [128, NC * LW], BF16)) for i in range(2)]
    sqbuf1 = ln_scope.enter_context(nc.sbuf_tensor("sqbuf1", [128, 2 * LW], BF16))
    lnb_m = ln_scope.enter_context(nc.sbuf_tensor("lnb_m", [128, LW], F32))
    lnb_a = ln_scope.enter_context(nc.sbuf_tensor("lnb_a", [128, LW], F32))
    lnb_b = ln_scope.enter_context(nc.sbuf_tensor("lnb_b", [128, LW], F32))
    tmpbuf = ln_scope.enter_context(nc.sbuf_tensor("tmpbuf", [128, 2 * LW], F32))

    lnp_scope = ExitStack()
    _ps_sum_full = lnp_scope.enter_context(nc.psum_tensor("ps_sum", [128, 512], F32))
    _ps_sq_full = lnp_scope.enter_context(nc.psum_tensor("ps_sq", [128, 512], F32))
    ps_sum = _ps_sum_full[:, 0:LW]
    ps_sq = _ps_sq_full[:, 0:LW]

    xslot = [Res(), Res()]
    xbslot = [Res(), Res()]
    ld_ops = {}
    ldb_ops = {}

    def x_ap(tc):
        return bass.AP(xt, tc * LW, [[S, 128], [128 * S, NC], [1, LW]])

    def get_ldx(tc):
        if tc in ld_ops:
            return ld_ops[tc]
        par = tc % 2
        ldx = p.dma("SP", lambda q, a=xf32[par], b=x_ap(tc): q.dma_start(a[:, :], b),
                    deps=xslot[par].write_deps(), sem_key=f"x{par}")
        xslot[par].wrote(ldx)
        ld_ops[tc] = ldx
        return ldx

    def get_ldb(tc):
        if tc in ldb_ops:
            return ldb_ops[tc]
        par = tc % 2
        ldb = p.dma("G", lambda q, a=xbf[par], b=x_ap(tc): q.dma_start(a[:, :], b),
                    deps=xbslot[par].write_deps() + (boot[:8] if tc == 0 else []),
                    sem_key=f"xb{par}")
        xbslot[par].wrote(ldb)
        ldb_ops[tc] = ldb
        return ldb

    ln1_fine = []
    r_sum1, r_sq1 = Res(), Res()
    sq_slot1 = [Res(), Res()]
    tmp_slot1 = [Res(), Res()]
    r_lnb = Res()

    get_ldx(0)
    get_ldb(0)
    get_ldx(1)
    get_ldb(1)
    for tc in range(NLC):
        par = tc % 2
        ldx = get_ldx(tc)
        ldb = get_ldb(tc)
        gf = lambda di, par=par: xf32[par][:, ts(di, LW)]
        gb = lambda di, par=par: xbf[par][:, ts(di, LW)]
        sum_mm = []
        sq_mm = []
        for di in range(NC):
            sl = di % 2
            sq_op = p.op("V", lambda v, di=di, sl=sl, gb=gb: v.tensor_tensor(
                sqbuf1[:, ts(sl, LW)], gb(di), gb(di), ALU.mult),
                deps=[ldb] + sq_slot1[sl].write_deps())
            sq_slot1[sl].wrote(sq_op)
            xbslot[par].read(sq_op)
            m1 = p.op("T", lambda t, di=di, gb=gb: t.matmul(
                ps_sum[:, :], ones_sq[:], gb(di),
                start=(di == 0), stop=(di == NC - 1)),
                deps=([ldb] + r_sum1.write_deps()) if di == 0 else [])
            xbslot[par].read(m1)
            m2 = p.op("T", lambda t, di=di, sl=sl: t.matmul(
                ps_sq[:, :], ones_sq[:], sqbuf1[:, ts(sl, LW)],
                start=(di == 0), stop=(di == NC - 1)),
                deps=[sq_op] + (r_sq1.write_deps() if di == 0 else []))
            sq_slot1[sl].read(m2)
            sum_mm.append(m1)
            sq_mm.append(m2)
        r_sum1.wrote(sum_mm[-1])
        r_sq1.wrote(sq_mm[-1])
        f1 = p.op("V", lambda v: v.tensor_scalar_mul(lnb_m[:, :], ps_sum[:, :], 1.0 / D),
                  deps=[sum_mm[-1]] + r_lnb.write_deps())
        r_sum1.read(f1)
        f2 = p.op("V", lambda v: v.tensor_scalar(lnb_a[:, :], ps_sq[:, :], 1.0 / D, EPS,
                                                 ALU.mult, ALU.add),
                  deps=[sq_mm[-1]] + r_lnb.write_deps())
        r_sq1.read(f2)
        f4 = p.op("V", lambda v: v.tensor_tensor(lnb_b[:, :], lnb_m[:, :], lnb_m[:, :],
                                                 ALU.mult), deps=[f1, f2])
        f5 = p.op("V", lambda v: v.tensor_sub(lnb_a[:, :], lnb_a[:, :], lnb_b[:, :]),
                  deps=[f4])
        f6 = p.op("V", lambda v: v.reciprocal(lnb_b[:, :], lnb_a[:, :]), deps=[f5])
        f7 = p.op("S", lambda s: s.activation(lnb_a[:, :], lnb_b[:, :], AF.Sqrt), deps=[f6])
        last = None
        for di in range(NC):
            sl = di % 2
            u1 = p.op("V", lambda v, di=di, sl=sl, gf=gf: v.tensor_sub(
                tmpbuf[:, ts(sl, LW)], gf(di), lnb_m[:, :]),
                deps=[f1, ldx, f7] + tmp_slot1[sl].write_deps())
            tmp_slot1[sl].wrote(u1)
            xslot[par].read(u1)
            u2 = p.op("V", lambda v, di=di, sl=sl, tc=tc: v.tensor_tensor(
                xn[:, di * S + tc * LW: di * S + (tc + 1) * LW],
                tmpbuf[:, ts(sl, LW)], lnb_a[:, :], ALU.mult),
                deps=[u1])
            tmp_slot1[sl].read(u2)
            last = u2
        r_lnb.wrote(last)
        ln1_fine.append(last)
        if tc + 2 < NLC:
            get_ldx(tc + 2)
            get_ldb(tc + 2)

    ln1_chunks = [ln1_fine[2 * i + 1] for i in range(4)]
    ln1_psum_last = (r_sum1.readers[-1:] + r_sq1.readers[-1:])
    lnp_scope.close()
    ln_scope.close()

    d_xn = dbg_t("xn", [128, NC * S], BF16)
    dump("SP", d_xn, xn[:, :], ln1_chunks)

    if stage == "ln1":
        xn_scope.close()
        oall_scope.close()
        est.close()
        return _finish(nc, p, dbg_ops, None), dbg

    att_scope = ExitStack()
    asb = lambda name, shape, dtype: att_scope.enter_context(nc.sbuf_tensor(name, shape, dtype))
    wqb = [asb(f"wqb{i}", [128, D], BF16) for i in range(2)]
    wkb = [asb(f"wkb{i}", [128, D], BF16) for i in range(2)]
    wvb = [asb(f"wvb{i}", [128, D], BF16) for i in range(2)]
    k_h = [asb(f"k_h{i}", [128, S], BF16) for i in range(2)]
    v_f = [asb(f"v_f{i}", [128, S], BF16) for i in range(2)]
    v_t = [asb(f"v_t{i}", [128, S], BF16) for i in range(2)]
    q_h = [asb(f"q_h{i}", [128, NOWN], BF16) for i in range(2)]
    pt_b = asb("pt", [128, NT * NOWN], BF16)
    sqb = asb("sqb", [128, NOWN], BF16)
    nrm = asb("nrm", [1, NOWN + 16], F32)
    mtmp = asb("mtmp", [1, NOWN], F32)
    mrow = [asb(f"mrow{i}", [1, NOWN], BF16) for i in range(2)]
    invb = asb("invb", [128, 512], F32)
    wstg = [asb(f"wstg{i}", [128, D], FP8) for i in range(2)]

    attp_scope = ExitStack()
    pR = attp_scope.enter_context(nc.psum_tensor("pR", [128, NOWN], F32))
    pP = [attp_scope.enter_context(nc.psum_tensor(f"pP{i}", [128, 512], F32)) for i in range(4)]
    rp = [Res(ln1_psum_last) for _ in range(4)]
    rR = Res(ln1_psum_last)
    rB67 = [Res(ln1_psum_last), Res(ln1_psum_last)]

    wslot = {k: [Res(), Res()] for k in ("wq", "wk", "wv")}
    rk_s = [Res(), Res()]
    rv_s = [Res(), Res()]
    rvt_s = [Res(), Res()]
    rq_s = [Res(), Res()]
    r_sqb = Res()
    r_mtmp = Res()
    r_nrm = Res()
    r_mrow_s = [Res(), Res()]
    r_invb = Res()
    r_pt = [Res() for _ in range(NT)]
    r_oall = Res()

    wbuf = {"wq": wqb, "wk": wkb, "wv": wvb}
    ln_guard = [ln1_fine[-1]]

    def load_w(kind, src, h):
        par = h % 2
        slot = wslot[kind][par]
        o = p.dma("G", lambda q, src=src, h=h, par=par, kind=kind: q.dma_start(
            wbuf[kind][par][:, :], src[h]),
            deps=slot.write_deps() + (boot_all + ln_guard if h <= 1 else []),
            sem_key=f"{kind}{par}")
        slot.wrote(o)
        return o

    wload = {}
    for h0 in range(min(2, n_heads)):
        for kind, src in (("wq", wq4), ("wk", wk4), ("wv", wv4)):
            wload[(kind, h0)] = load_w(kind, src, h0)

    pre_jobs = []
    for m in range(NH):
        pre_jobs.append((wg0[m], bass.AP(wg8, (m * 2 + 0) * 128 * D, [[D, 128], [1, D]])))
        pre_jobs.append((wg1[m], bass.AP(wg8, (m * 2 + 1) * 128 * D, [[D, 128], [1, D]])))
        pre_jobs.append((wl[m], wl8[m]))
    wg8_done = [[None, None] for _ in range(NH)]
    wl8_done = [None] * NH
    wstg_slot = [Res(), Res()]
    _pre_i = [0]

    def emit_prepass(njobs):
        for _ in range(njobs):
            i = _pre_i[0]
            if i >= len(pre_jobs):
                return
            _pre_i[0] += 1
            src, dst = pre_jobs[i]
            sl = i % 2
            ld = p.dma("G", lambda q, src=src, sl=sl: q.dma_start(wstg[sl][:, :], src),
                       deps=wstg_slot[sl].write_deps() + (boot_all + ln_guard if i <= 1 else []),
                       sem_key=f"pre{sl}")
            st = p.dma("SP", lambda q, dst=dst, sl=sl: q.dma_start(dst, wstg[sl][:, :]),
                       deps=[ld], sem_key=f"pres{sl}")
            wstg_slot[sl].wrote(st)
            kind = i % 3
            m = i // 3
            if kind == 0:
                wg8_done[m][0] = st
            elif kind == 1:
                wg8_done[m][1] = st
            else:
                wl8_done[m] = st

    qstart = [128 * (kc // 2) for kc in range(NT)]
    head_out_ops = []
    d_k0 = dbg_t("k0", [128, S], BF16)
    d_vt0 = dbg_t("vt0", [128, S], BF16)
    d_q0 = dbg_t("q0", [128, NOWN], BF16)
    d_mrow0 = dbg_t("mrow0", [1, NOWN], BF16)
    d_nrm0 = dbg_t("nrm0", [1, NOWN + 16], F32)
    d_pt0 = dbg_t("pt0", [128, NT * NOWN], BF16)

    def emit_proj_bounds(h):
        par = h % 2
        kb, vb, qb = k_h[par], v_f[par], q_h[par]
        rk, rv, rq = rk_s[par], rv_s[par], rq_s[par]
        r_mrow = r_mrow_s[par]

        k_ev = []
        k_mm_last = None
        for tc in range(4):
            bank = tc % 2
            mms = []
            for di in range(NC):
                mm = p.op("T", lambda t, di=di, tc=tc, bank=bank, par=par: t.matmul(
                    pP[bank][:, :], wkb[par][:, ts(di)],
                    xn[:, di * S + tc * 512: di * S + (tc + 1) * 512],
                    start=(di == 0), stop=(di == NC - 1)),
                    deps=([wload[("wk", h)], ln1_chunks[tc]] + rp[bank].write_deps()
                          + (rk.write_deps() if tc < 2 else [])) if di == 0 else [])
                mms.append(mm)
            rp[bank].wrote(mms[-1])
            k_mm_last = mms[-1]
            ev = p.op("S", lambda s, tc=tc, bank=bank, kb=kb, h=h: s.activation(
                kb[:, tc * 512:(tc + 1) * 512], pP[bank][:, :], AF.Identity,
                bias=bk_sb[:, h:h + 1]),
                deps=[mms[-1]] + (bias_ld if h == 0 and tc == 0 else []))
            rp[bank].read(ev)
            k_ev.append(ev)
        rk.wrote(k_ev[-1])
        wslot["wk"][par].read(k_mm_last)

        v_ev = []
        v_mm_last = None
        for tc in range(4):
            bank = 2 + tc % 2
            mms = []
            for di in range(NC):
                mm = p.op("T", lambda t, di=di, tc=tc, bank=bank, par=par: t.matmul(
                    pP[bank][:, :], wvb[par][:, ts(di)],
                    xn[:, di * S + tc * 512: di * S + (tc + 1) * 512],
                    start=(di == 0), stop=(di == NC - 1)),
                    deps=([wload[("wv", h)], ln1_chunks[tc]] + rp[bank].write_deps()
                          + (rv.write_deps() if tc < 2 else [])) if di == 0 else [])
                mms.append(mm)
            rp[bank].wrote(mms[-1])
            v_mm_last = mms[-1]
            ev = p.op("S", lambda s, tc=tc, bank=bank, vb=vb, h=h: s.activation(
                vb[:, tc * 512:(tc + 1) * 512], pP[bank][:, :], AF.Identity,
                bias=bv_sb[:, h:h + 1]),
                deps=[mms[-1]] + (bias_ld if h == 0 and tc == 0 else []))
            rp[bank].read(ev)
            v_ev.append(ev)
        rv.wrote(v_ev[-1])
        wslot["wv"][par].read(v_mm_last)

        q_ev = []
        q_mm_last = None
        for half in range(2):
            bank = half
            mms = []
            for di in range(NC):
                rhs = bass.AP(xn, di * S + 128 + half * 1024, [[NC * S, 128], [256, 4], [1, 128]])
                mm = p.op("T", lambda t, di=di, bank=bank, rhs=rhs, par=par: t.matmul(
                    pP[bank][:, :], wqb[par][:, ts(di)], rhs,
                    start=(di == 0), stop=(di == NC - 1)),
                    deps=([wload[("wq", h)], ln1_chunks[3]] + rp[bank].write_deps()
                          + (rq.write_deps() if half == 0 else [])) if di == 0 else [])
                mms.append(mm)
            rp[bank].wrote(mms[-1])
            q_mm_last = mms[-1]
            ev = p.op("S", lambda s, half=half, bank=bank, qb=qb, h=h: s.activation(
                qb[:, half * 512:(half + 1) * 512], pP[bank][:, :], AF.Identity,
                bias=bq_sb[:, h:h + 1]),
                deps=[mms[-1]] + (bias_ld if h == 0 and half == 0 else []))
            rp[bank].read(ev)
            q_ev.append(ev)
        rq.wrote(q_ev[-1])
        wslot["wq"][par].read(q_mm_last)

        if h + 2 < n_heads:
            for kind, src in (("wq", wq4), ("wk", wk4), ("wv", wv4)):
                wload[(kind, h + 2)] = load_w(kind, src, h + 2)

        kn_acc = []
        ksq_mm = []
        for tc in range(4):
            sl = tc % 2
            ksq = p.op("V", lambda v, kb=kb, tc=tc, sl=sl: v.tensor_tensor(
                sqb[:, ts(sl, 512)], kb[:, tc * 512:(tc + 1) * 512],
                kb[:, tc * 512:(tc + 1) * 512], ALU.mult),
                deps=[k_ev[tc]] + ([ksq_mm[tc - 2]] if tc >= 2 else r_sqb.write_deps()))
            rk.read(ksq)
            bank = 2 + tc % 2
            mm = p.op("T", lambda t, sl=sl, bank=bank: t.matmul(
                pP[bank][:, :], ones_sq[:], sqb[:, ts(sl, 512)],
                start=True, stop=True),
                deps=[ksq] + rp[bank].write_deps())
            rp[bank].wrote(mm)
            ksq_mm.append(mm)
            if tc == 0:
                ev = p.op("V", lambda v, bank=bank: v.tensor_copy(
                    mtmp[:, 0:512], pP[bank][0:1, :]),
                    deps=[mm] + r_mtmp.write_deps())
            else:
                ev = p.op("V", lambda v, bank=bank: v.tensor_tensor(
                    mtmp[:, 0:512], mtmp[:, 0:512], pP[bank][0:1, :], ALU.add),
                    deps=[mm, kn_acc[-1]])
            rp[bank].read(ev)
            kn_acc.append(ev)
        r_mtmp.wrote(kn_acc[-1])
        ksum = p.op("V", lambda v: v.reduce_sum(nrm[:, NOWN + 9:NOWN + 10],
                                                mtmp[:, 0:512], axis=AX.X),
                    deps=[kn_acc[-1]] + r_nrm.write_deps())
        r_mtmp.read(ksum)
        qsq = p.op("V", lambda v, qb=qb: v.tensor_tensor(sqb[:, 0:NOWN], qb[:, :], qb[:, :],
                                                         ALU.mult),
                   deps=[q_ev[-1], ksq_mm[-1], ksq_mm[-2]])
        rq.read(qsq)
        r_sqb.wrote(qsq)
        qn_ev = []
        for half in range(2):
            bank = half
            mm = p.op("T", lambda t, half=half, bank=bank: t.matmul(
                pP[bank][:, :], ones_sq[:], sqb[:, half * 512:(half + 1) * 512],
                start=True, stop=True),
                deps=[qsq] + rp[bank].write_deps())
            rp[bank].wrote(mm)
            r_sqb.read(mm)
            ev = p.op("V", lambda v, half=half, bank=bank: v.tensor_copy(
                nrm[:, half * 512:(half + 1) * 512], pP[bank][0:1, :]),
                deps=[mm] + (r_nrm.write_deps() if half == 0 else []))
            rp[bank].read(ev)
            qn_ev.append(ev)
        t1 = p.op("S", lambda s: s.activation(mtmp[:, :], nrm[:, 0:NOWN], AF.Ln,
                                              scale=nrm[:, NOWN + 9:NOWN + 10]),
                  deps=qn_ev + [ksum] + r_mtmp.write_deps())
        r_mtmp.wrote(t1)
        t2 = p.op("S", lambda s: s.activation(nrm[:, 0:NOWN], mtmp[:, :], AF.Exp,
                                              scale=0.5, bias=lnc[:, 0:1]), deps=[t1])
        r_mtmp.read(t2)
        mr = p.op("V", lambda v, par=par: v.tensor_scalar_mul(mrow[par][:, :], nrm[:, 0:NOWN], -1.0),
                  deps=[t2] + r_mrow.write_deps())
        r_mrow.wrote(mr)
        r_nrm.wrote(t2)
        r_nrm.read(mr)
        emit_prepass(12)
        return dict(k_ev=k_ev, v_ev=v_ev, q_ev=q_ev, mr=mr)

    def emit_attn_core(h, pb):
        par = h % 2
        kb, vb, vt, qb = k_h[par], v_f[par], v_t[par], q_h[par]
        rk, rv, rvt, rq = rk_s[par], rv_s[par], rvt_s[par], rq_s[par]
        r_mrow = r_mrow_s[par]
        k_ev, v_ev, q_ev, mr = pb["k_ev"], pb["v_ev"], pb["q_ev"], pb["mr"]

        with nc.psum_tensor(f"pT0_{h}", [128, 1024], BF16) as pT0, \
             nc.psum_tensor(f"pT1_{h}", [128, 1024], BF16) as pT1:
            pTs = [pT0, pT1]
            tr_ev = []
            for kc in range(NT):
                bk_ = kc % 2
                trm = p.op("T", lambda t, kc=kc, bk_=bk_, vb=vb, pTs=pTs: t.transpose(
                    pTs[bk_][:, 0:128], vb[:, ts(kc)], ident[:]),
                    deps=[v_ev[kc // 4]] + rB67[bk_].write_deps())
                rB67[bk_].wrote(trm)
                rv.read(trm)
                ev = p.op("V", lambda v, kc=kc, bk_=bk_, vt=vt, pTs=pTs: v.tensor_copy(
                    vt[:, ts(kc)], pTs[bk_][:, 0:128]),
                    deps=[trm] + (rvt.write_deps() if kc < 2 else []))
                rB67[bk_].read(ev)
                tr_ev.append(ev)
            rvt.wrote(tr_ev[-1])

        with nc.psum_tensor(f"pO0_{h}", [128, 512], F32) as pO0, \
             nc.psum_tensor(f"pO1_{h}", [128, 512], F32) as pO1:
            pOs = [pO0, pO1]
            exp_done = {}
            pv_ops = []
            r_ops = []

            def emit_st(kc, kb=kb, qb=qb, par=par):
                qs = qstart[kc]
                pair = (0, 1) if (kc % 2 == 0) else (2, 3)
                segs = []
                for half in range(2):
                    lo = max(qs, half * 512)
                    hi = (half + 1) * 512
                    if lo >= hi:
                        continue
                    w_ = hi - lo
                    bank = pair[half]
                    mm1 = p.op("T", lambda t, kc=kc, lo=lo, w_=w_, bank=bank: t.matmul(
                        pP[bank][:, 0:w_], kb[:, ts(kc)], qb[:, lo:lo + w_],
                        start=True, stop=False),
                        deps=[k_ev[kc // 4], q_ev[lo // 512]] + rp[bank].write_deps())
                    mm2 = p.op("T", lambda t, kc=kc, lo=lo, w_=w_, bank=bank: t.matmul(
                        pP[bank][:, 0:w_], ones_r[:], mrow[par][:, lo:lo + w_],
                        start=False, stop=True), deps=[mr])
                    rp[bank].wrote(mm2)
                    rk.read(mm1)
                    rq.read(mm1)
                    r_mrow.read(mm2)
                    bias_ap = pmb_sb[:, 0:1] if kc == 0 else zcol[:, 0:1]
                    evx = p.op("S", lambda s, kc=kc, lo=lo, w_=w_, bank=bank, bias_ap=bias_ap: s.activation(
                        pt_b[:, kc * NOWN + lo: kc * NOWN + lo + w_], pP[bank][:, 0:w_],
                        AF.Exp, bias=bias_ap, scale=S2),
                        deps=[mm2] + (r_pt[kc].write_deps() if lo == qs else []))
                    rp[bank].read(evx)
                    segs.append(evx)
                last = segs[-1]
                if kc % 2 == 1:
                    i0 = (kc - 1) // 2
                    dm = p.op("V", lambda v, kc=kc, i0=i0: v.tensor_tensor(
                        pt_b[:, kc * NOWN + i0 * 128: kc * NOWN + (i0 + 1) * 128],
                        pt_b[:, kc * NOWN + i0 * 128: kc * NOWN + (i0 + 1) * 128],
                        tri[:], ALU.mult), deps=segs)
                    last = dm
                r_pt[kc].wrote(last)
                exp_done[kc] = last

            def emit_pv(kc, vt=vt):
                qs = qstart[kc]
                last = exp_done[kc]
                for half in range(2):
                    lo = max(qs, half * 512)
                    hi = (half + 1) * 512
                    if lo >= hi:
                        continue
                    w_ = hi - lo
                    mmo = p.op("T", lambda t, kc=kc, lo=lo, w_=w_, half=half, pOs=pOs: t.matmul(
                        pOs[half][:, lo - half * 512: lo - half * 512 + w_],
                        vt[:, ts(kc)], pt_b[:, kc * NOWN + lo: kc * NOWN + lo + w_],
                        start=(kc == 0), stop=(kc == NT - 1)),
                        deps=[last, tr_ev[kc]] + (rB67[half].write_deps() if kc == 0 else []))
                    if kc == 0:
                        rB67[half].wrote(mmo)
                    else:
                        rB67[half].touch(mmo)
                    rvt.read(mmo)
                    r_pt[kc].read(mmo)
                    pv_ops.append(mmo)
                    mmr = p.op("T", lambda t, kc=kc, lo=lo, w_=w_: t.matmul(
                        pR[:, lo:lo + w_], ones_sq[:], pt_b[:, kc * NOWN + lo: kc * NOWN + lo + w_],
                        start=(kc == 0), stop=(kc == NT - 1)),
                        deps=[last] + (rR.write_deps() if kc == 0 else []))
                    if kc == 0:
                        rR.wrote(mmr)
                    else:
                        rR.touch(mmr)
                    r_pt[kc].read(mmr)
                    r_ops.append(mmr)

            emit_st(0)
            emit_st(1)
            for kc in range(2, NT):
                emit_st(kc)
                emit_pv(kc - 2)
            emit_pv(NT - 2)
            emit_pv(NT - 1)

            o_evs = []
            for half in range(2):
                rcp = p.op("V", lambda v, half=half: v.reciprocal(
                    invb[:, :], pR[:, half * 512:(half + 1) * 512]),
                    deps=[r_ops[-1]] + r_invb.write_deps())
                rR.read(rcp)
                r_invb.wrote(rcp)
                oe = p.op("V", lambda v, half=half, h=h, pOs=pOs: v.tensor_tensor(
                    o_all[:, h * NOWN + half * 512: h * NOWN + (half + 1) * 512],
                    pOs[half][:, :], invb[:, :], ALU.mult),
                    deps=[rcp, pv_ops[-1], pv_ops[-2] if len(pv_ops) >= 2 else None])
                rB67[half].read(oe)
                r_invb.read(oe)
                o_evs.append(oe)
            head_out_ops.append(o_evs[-1])

        if h == 0:
            dump("SP", d_k0, k_h[0][:, :], [k_ev[-1]])
            rk.read(dump("SP", d_vt0, v_t[0][:, :], [tr_ev[-1]]))
            dump("SP", d_q0, q_h[0][:, :], [q_ev[-1]])
            dump("SP", d_mrow0, mrow[0][:, :], [mr])
            dump("SP", d_nrm0, nrm[:, :], [mr])
            if d_pt0 is not None:
                o = dump("SP", d_pt0, pt_b[:, :], [exp_done[kc] for kc in range(NT)])
                for kc in range(NT):
                    r_pt[kc].read(o)

    pb_next = emit_proj_bounds(0)
    for h in range(n_heads):
        pb_cur = pb_next
        if h + 1 < n_heads:
            pb_next = emit_proj_bounds(h + 1)
        emit_attn_core(h, pb_cur)
    emit_prepass(len(pre_jobs))

    att_last = head_out_ops[-1]
    d_oall = dbg_t("o_all", [128, H * NOWN], BF16)
    do_ = dump("SP", d_oall, o_all[:, :], [att_last])
    r_oall.read(do_)

    attp_scope.close()
    att_scope.close()
    xn_scope.close()

    if stage == "attn":
        oall_scope.close()
        est.close()
        return _finish(nc, p, dbg_ops, None), dbg

    x1 = sb("x1", [128, NC * NOWN], F32, side="right")
    wo_scope = ExitStack()
    wob = [wo_scope.enter_context(nc.sbuf_tensor(f"wob{i}", [128, D], BF16)) for i in range(2)]
    x_own = wo_scope.enter_context(nc.sbuf_tensor("x_own", [128, NC * NOWN], F32))

    wop_scope = ExitStack()
    pW = [wop_scope.enter_context(nc.psum_tensor(f"pW{i}", [128, 512], F32)) for i in range(4)]
    rW = [Res([att_last]) for _ in range(4)]

    xo_guard = [att_last] + (dbg_ops[-1:] if dbg_ops else [])
    xo_l = []
    for di in range(NC):
        in_ap = bass.AP(xt, di * 128 * S + 128, [[S, 128], [256, 8], [1, 128]])
        xo_l.append(p.dma("SP", lambda q, di=di, b=in_ap: q.dma_start(
            x_own[:, di * NOWN:(di + 1) * NOWN], b), deps=xo_guard, sem_key=f"xo{di % 2}"))

    wo_slot = [Res(xo_guard), Res(xo_guard)]

    def load_wo(m):
        par = m % 2
        o = p.dma("G", lambda q, m=m, par=par: q.dma_start(wob[par][:, :], wo4[m]),
                  deps=wo_slot[par].write_deps(), sem_key=f"wo{par}")
        wo_slot[par].wrote(o)
        return o

    wo_l = {0: load_wo(0), 1: load_wo(1)}
    x1_ready = {}
    for m in range(NC):
        par = m % 2
        banks = (0, 1) if par == 0 else (2, 3)
        last_mm_b = None
        for half in range(2):
            bank = banks[half]
            mms = []
            for di in range(NC):
                mm = p.op("T", lambda t, m=m, di=di, half=half, bank=bank, par=par: t.matmul(
                    pW[bank][:, :], wob[par][:, ts(di)],
                    o_all[:, di * NOWN + half * 512: di * NOWN + (half + 1) * 512],
                    start=(di == 0), stop=False),
                    deps=([wo_l[m], att_last] + rW[bank].write_deps()) if di == 0 else [])
                mms.append(mm)
            if m < 2 and half == 0:
                r_oall.read(mms[0])
            mmb = p.op("T", lambda t, m=m, bank=bank: t.matmul(
                pW[bank][:, :], bo_sb[:, ts(m)], ones_row[:, :], start=False, stop=True),
                deps=[bias_ld[-1]] if m == 0 else [])
            rW[bank].wrote(mmb)
            wo_slot[par].read(mmb)
            last_mm_b = mmb
            ev = p.op("V", lambda v, m=m, half=half, bank=bank: v.tensor_tensor(
                x1[:, m * NOWN + half * 512: m * NOWN + (half + 1) * 512],
                pW[bank][:, :], x_own[:, m * NOWN + half * 512: m * NOWN + (half + 1) * 512],
                ALU.add), deps=[mmb, xo_l[m]])
            rW[bank].read(ev)
            x1_ready[(m, half)] = ev
        if m + 2 < NC:
            wo_l[m + 2] = load_wo(m + 2)

    x1_all = [x1_ready[(m, hf)] for m in range(NC) for hf in range(2)]
    d_x1 = dbg_t("x1", [128, NC * NOWN], F32)
    dump("SP", d_x1, x1[:, :], x1_all)

    wop_scope.close()
    wo_scope.close()
    oall_scope.close()
    if stage == "wo":
        est.close()
        return _finish(nc, p, dbg_ops, None), dbg

    xn2 = sb("xn2", [128, NC * NOWN], FP8)
    act_b = sb("act", [128, NH * NOWN], FP8)
    ffn_scope = ExitStack()
    fsb = lambda name, shape, dtype: ffn_scope.enter_context(nc.sbuf_tensor(name, shape, dtype))
    wgb = [fsb(f"wgb{i}", [128, 2 * D], FP8) for i in range(2)]
    wlb = [fsb(f"wlb{i}", [128, 4 * D], FP8) for i in range(2)]
    tgb = fsb("tgb", [128, 2 * 512], BF16)
    ostg = fsb("ostg", [128, 16 * 256], F32)
    l2sq = fsb("l2sq", [128, 2 * 512], BF16)
    l2bf = fsb("l2bf", [128, 2 * 512], BF16)
    l2b_m = fsb("l2b_m", [128, 512], F32)
    l2b_a = fsb("l2b_a", [128, 512], F32)
    l2tmp = fsb("l2tmp", [128, 2 * 512], F32)
    l2b_b = l2tmp[:, 0:512]

    ln2p_scope = ExitStack()
    p2_sum = ln2p_scope.enter_context(nc.psum_tensor("p2_sum", [128, 512], F32))
    p2_sq = ln2p_scope.enter_context(nc.psum_tensor("p2_sq", [128, 512], F32))
    wo_psum_last = [rW[i].readers[-1] for i in range(4) if rW[i].readers]
    r_sum2, r_sq2 = (Res(wo_psum_last) for _ in range(2))
    r_lnb2 = Res()
    sq_slot2 = [Res(), Res()]
    tmp_slot2 = [Res(), Res()]
    l2bf_slot = [Res(), Res()]

    xn2_done = []
    for tc in range(2):
        sum_mm = []
        sq_mm = []
        for di in range(NC):
            sl = di % 2
            src = l2bf[:, ts(sl, 512)]
            cp = p.op("S", lambda s, di=di, sl=sl, tc=tc: s.activation(
                l2bf[:, ts(sl, 512)],
                x1[:, di * NOWN + tc * 512: di * NOWN + (tc + 1) * 512], AF.Identity),
                deps=[x1_ready[(di, tc)]] + l2bf_slot[sl].write_deps())
            l2bf_slot[sl].wrote(cp)
            sq_op = p.op("V", lambda v, sl=sl, src=src: v.tensor_tensor(
                l2sq[:, ts(sl, 512)], src, src, ALU.mult),
                deps=[cp] + sq_slot2[sl].write_deps())
            sq_slot2[sl].wrote(sq_op)
            l2bf_slot[sl].read(sq_op)
            m1 = p.op("T", lambda t, di=di, src=src: t.matmul(
                p2_sum[:, :], ones_sq[:], src, start=(di == 0), stop=(di == NC - 1)),
                deps=([cp] + r_sum2.write_deps()) if di == 0 else [cp])
            m2 = p.op("T", lambda t, di=di, sl=sl: t.matmul(
                p2_sq[:, :], ones_sq[:], l2sq[:, ts(sl, 512)],
                start=(di == 0), stop=(di == NC - 1)),
                deps=[sq_op] + (r_sq2.write_deps() if di == 0 else []))
            sq_slot2[sl].read(m2)
            l2bf_slot[sl].read(m1)
            sum_mm.append(m1)
            sq_mm.append(m2)
        r_sum2.wrote(sum_mm[-1])
        r_sq2.wrote(sq_mm[-1])
        f1 = p.op("V", lambda v: v.tensor_scalar_mul(l2b_m[:, :], p2_sum[:, :], 1.0 / D),
                  deps=[sum_mm[-1]] + r_lnb2.write_deps())
        r_sum2.read(f1)
        f2 = p.op("V", lambda v: v.tensor_scalar(l2b_a[:, :], p2_sq[:, :], 1.0 / D, EPS,
                                                 ALU.mult, ALU.add),
                  deps=[sq_mm[-1]] + r_lnb2.write_deps())
        r_sq2.read(f2)
        f4 = p.op("V", lambda v: v.tensor_tensor(l2b_b, l2b_m[:, :], l2b_m[:, :],
                                                 ALU.mult), deps=[f1, f2])
        f5 = p.op("V", lambda v: v.tensor_sub(l2b_a[:, :], l2b_a[:, :], l2b_b),
                  deps=[f4])
        f6 = p.op("V", lambda v: v.reciprocal(l2b_b, l2b_a[:, :]), deps=[f5])
        f7 = p.op("S", lambda s: s.activation(l2b_a[:, :], l2b_b, AF.Sqrt), deps=[f6])
        last = None
        for di in range(NC):
            sl = di % 2
            u1 = p.op("V", lambda v, di=di, sl=sl, tc=tc: v.tensor_sub(
                l2tmp[:, ts(sl, 512)],
                x1[:, di * NOWN + tc * 512: di * NOWN + (tc + 1) * 512], l2b_m[:, :]),
                deps=[f1, f7] + tmp_slot2[sl].write_deps())
            tmp_slot2[sl].wrote(u1)
            u2 = p.op("V", lambda v, di=di, sl=sl, tc=tc: v.tensor_tensor(
                xn2[:, di * NOWN + tc * 512: di * NOWN + (tc + 1) * 512],
                l2tmp[:, ts(sl, 512)], l2b_a[:, :], ALU.mult),
                deps=[u1])
            tmp_slot2[sl].read(u2)
            last = u2
        r_lnb2.wrote(last)
        xn2_done.append(last)
    ln2_psum_last = (r_sum2.readers[-1:] + r_sq2.readers[-1:] + [xn2_done[-1]])
    ln2p_scope.close()
    d_xn2 = dbg_t("xn2", [128, NC * NOWN], F32)
    dump("G", d_xn2, xn2[:, :], xn2_done)

    mm1p_scope = ExitStack()
    pG = [mm1p_scope.enter_context(nc.psum_tensor(f"pG{i}", [128, 512], F32)) for i in range(2)]
    pF = [mm1p_scope.enter_context(nc.psum_tensor(f"pF{i}", [128, 512], F32)) for i in range(2)]
    rG = [Res(ln2_psum_last), Res(ln2_psum_last)]
    rF = [Res(ln2_psum_last), Res(ln2_psum_last)]

    ffn_guard = [x1_all[-1]] + (dbg_ops[-1:] if dbg_ops else [])
    wg_slot = [Res(ffn_guard), Res(ffn_guard)]

    def load_wg(m):
        par = m % 2
        in_ap = bass.AP(wg8, m * 2 * 128 * D, [[D, 128], [128 * D, 2], [1, D]])
        d = p.dma("SP", lambda q, b=in_ap, par=par: q.dma_start(wgb[par][:, :], b),
                  deps=wg_slot[par].write_deps() + wg8_done[m], sem_key=f"wg{par}a")
        wg_slot[par].wrote(d)
        return (d, d)

    wg_l = {0: load_wg(0), 1: load_wg(1)}
    act_ready = [None] * NH
    last_ge = None

    NDP = NC // 2

    def mm1_aps(par, gsel, dp, half):
        lhsT = bass.AP(wgb[par], gsel * D + 2 * dp * 128, [[2 * D, 128], [128, 2], [1, 128]])
        rhs = bass.AP(xn2, (2 * dp) * NOWN + half * 512, [[NC * NOWN, 128], [NOWN, 2], [1, 512]])
        return lhsT, rhs

    for m in range(NH):
        par = m % 2
        d1, d2 = wg_l[m]
        for half in range(2):
            mg = []
            for dp in range(NDP):
                mm = p.op("T", lambda t, dp=dp, half=half, par=par: t.matmul(
                    pG[par][:, :], *mm1_aps(par, 0, dp, half),
                    start=(dp == 0), stop=(dp == NDP - 1),
                    perf_mode=mybir.MatmulPerfMode.DoubleRow),
                    deps=([d1, xn2_done[-1]] + rG[par].write_deps()) if dp == 0 else [])
                mg.append(mm)
            rG[par].wrote(mg[-1])
            mf = []
            for dp in range(NDP):
                mm = p.op("T", lambda t, dp=dp, half=half, par=par: t.matmul(
                    pF[par][:, :], *mm1_aps(par, 1, dp, half),
                    start=(dp == 0), stop=(dp == NDP - 1),
                    perf_mode=mybir.MatmulPerfMode.DoubleRow),
                    deps=([d2] + rF[par].write_deps()) if dp == 0 else [])
                mf.append(mm)
            rF[par].wrote(mf[-1])
            ge = p.op("S", lambda s, half=half, par=par: s.activation(
                tgb[:, ts(par, 512)], pG[par][:, :], AF.Gelu_apprx_tanh), deps=[mg[-1]])
            rG[par].read(ge)
            last_ge = ge
            mu = p.op("V", lambda v, m=m, half=half, par=par: v.tensor_tensor(
                act_b[:, m * NOWN + half * 512: m * NOWN + (half + 1) * 512],
                tgb[:, ts(par, 512)], pF[par][:, :], ALU.mult), deps=[ge, mf[-1]])
            rF[par].read(mu)
            wg_slot[par].read(mu)
            act_ready[m] = mu
        if m + 2 < NH:
            wg_l[m + 2] = load_wg(m + 2)

    d_act = dbg_t("act", [128, NH * NOWN], F32)
    dump("G", d_act, act_b[:, :], [act_ready[NH - 1]])
    mm1_psum_last = [op for op in (last_ge, act_ready[NH - 1]) if op is not None]
    mm1p_scope.close()

    mm2p_scope = ExitStack()
    pM = mm2p_scope.enter_context(nc.psum_tensor("pM", [128, NC * 256], F32))
    rM = Res(mm1_psum_last)

    wl_slot = [Res(), Res()]
    r_ostg = [Res(), Res()]
    out_sts = []

    def load_wl8(qt, blk):
        par = blk % 2
        in_ap = bass.AP(wl8, blk * 4 * 128 * D, [[D, 128], [128 * D, 4], [1, D]])
        o = p.dma("G", lambda q, b=in_ap, par=par: q.dma_start(wlb[par][:, :], b),
                  deps=(wl_slot[par].write_deps() + ffn_guard
                        + [wl8_done[blk * 4 + j] for j in range(4)]),
                  sem_key=f"wlq{par}")
        wl_slot[par].wrote(o)
        return o

    pending = {(0, 0): load_wl8(0, 0), (0, 1): load_wl8(0, 1)}
    NBLK = 16
    for qt in range(4):
        quarter_last_mm = None
        for blk in range(NBLK):
            par = blk % 2
            ld = pending.pop((qt, blk))
            blk_last = None
            for hi in (0, 2):
                hh = blk * 4 + hi
                for f in range(NC):
                    st_flag = (hh == 0 and f % 2 == 0)
                    lhsT = bass.AP(wlb[par], hi * D + f * 128, [[4 * D, 128], [D, 2], [1, 128]])
                    rhs = bass.AP(act_b, hh * NOWN + qt * 256, [[NH * NOWN, 128], [NOWN, 2], [1, 256]])
                    mm = p.op("T", lambda t, f=f, st_flag=st_flag, hh=hh, lhsT=lhsT, rhs=rhs: t.matmul(
                        pM[:, f * 256:(f + 1) * 256], lhsT, rhs,
                        start=st_flag, stop=(hh == NH - 2),
                        perf_mode=mybir.MatmulPerfMode.DoubleRow),
                        deps=([ld, act_ready[hh + 1]] + (rM.write_deps() if hh == 0 else [])) if f == 0 else [])
                    blk_last = mm
            wl_slot[par].read(blk_last)
            nxt = (qt, blk + 2) if blk + 2 < NBLK else ((qt + 1, blk + 2 - NBLK) if qt < 3 else None)
            if nxt is not None and nxt not in pending:
                pending[nxt] = load_wl8(*nxt)
            quarter_last_mm = blk_last
        rM.wrote(quarter_last_mm)
        evs = []
        for fg in range(2):
            for f8 in range(8):
                f = fg * 8 + f8
                ev = p.op("V", lambda v, qt=qt, f=f, f8=f8, fg=fg: v.tensor_tensor(
                    ostg[:, fg * 2048 + f8 * 256: fg * 2048 + (f8 + 1) * 256],
                    pM[:, f * 256:(f + 1) * 256],
                    x1[:, f * NOWN + qt * 256: f * NOWN + qt * 256 + 256], ALU.add),
                    deps=[quarter_last_mm] + (r_ostg[fg].write_deps() if f8 == 0 else []))
                rM.read(ev)
                evs.append(ev)
            out_ap = bass.AP(out, fg * 8 * 128 * NOWN + qt * 256,
                             [[NOWN, 128], [128 * NOWN, 8], [1, 256]])
            st = p.dma("SP", lambda q, a=out_ap, fg=fg: q.dma_start(
                a, ostg[:, fg * 2048:(fg + 1) * 2048]),
                deps=evs[-8:], sem_key="out")
            r_ostg[fg].wrote(evs[-1])
            r_ostg[fg].read(st)
            out_sts.append(st)

    mm2p_scope.close()
    ffn_scope.close()
    est.close()
    return _finish(nc, p, dbg_ops, out_sts), dbg


def _finish(nc, p, dbg_ops, out_sts):
    keys = p.lower()
    with ExitStack() as sctx:
        sems = {k: sctx.enter_context(nc.semaphore(f"s_{k}")) for k in sorted(keys)}
        with nc.Block() as block:
            @block.tensor
            def _(t):
                p.emit_engine("T", t, sems)

            @block.vector
            def _(v):
                p.emit_engine("V", v, sems)

            @block.scalar
            def _(s):
                p.emit_engine("S", s, sems)

            @block.gpsimd
            def _(g):
                p.emit_engine("G", g, sems)

            @block.sync
            def _(sp):
                p.emit_engine("SP", sp, sems)
                if out_sts:
                    sp.wait_ge(sems["out"], p.final_counts["out"])
                if dbg_ops and "dbg" in sems:
                    sp.wait_ge(sems["dbg"], p.final_counts["dbg"])
    return nc



def _prep_inputs(L0_attn_X, L0_attn_Wq, L0_attn_bq, L0_attn_Wk, L0_attn_bk,
                 L0_attn_Wv, L0_attn_bv, L0_attn_Wo, L0_attn_bo,
                 L0_ffn_w_gating, L0_ffn_w_linear):
    X = np.ascontiguousarray(np.asarray(L0_attn_X, dtype=np.float32))
    Wq = np.asarray(L0_attn_Wq, dtype=np.float32)
    Wk = np.asarray(L0_attn_Wk, dtype=np.float32)
    Wv = np.asarray(L0_attn_Wv, dtype=np.float32)
    Wo = np.asarray(L0_attn_Wo, dtype=np.float32)
    wg = np.asarray(L0_ffn_w_gating, dtype=np.float32)
    wlin = np.asarray(L0_ffn_w_linear, dtype=np.float32)

    def head_tiles(W):
        t = W.reshape(H, 128, NC, 128)
        return np.ascontiguousarray(t.transpose(0, 3, 2, 1)).reshape(H, 128, D)

    wq4 = head_tiles(Wq)
    wk4 = head_tiles(Wk)
    wv4 = head_tiles(Wv)
    wo4 = head_tiles(Wo)

    def gate_tiles(Wg):
        t = Wg.reshape(NC, 128, NH, 128)
        return np.ascontiguousarray(t.transpose(2, 1, 0, 3)).reshape(NH, 128, D)

    wg0 = gate_tiles(wg[0])
    wg1 = gate_tiles(wg[1])
    wl_t = np.ascontiguousarray(wlin.reshape(NH, 128, D))

    bq_r = np.ascontiguousarray(np.asarray(L0_attn_bq, np.float32).reshape(H, 128).T)
    bk_r = np.ascontiguousarray(np.asarray(L0_attn_bk, np.float32).reshape(H, 128).T)
    bv_r = np.ascontiguousarray(np.asarray(L0_attn_bv, np.float32).reshape(H, 128).T)
    bo_r = np.ascontiguousarray(np.asarray(L0_attn_bo, np.float32).reshape(1, D))

    in_maps = []
    for c in range(8):
        b = c // 2
        odd = c % 2
        XT = np.ascontiguousarray(X[b].T)
        if odd:
            xt_c = XT
        else:
            xt_c = np.ascontiguousarray(
                np.concatenate([np.zeros((D, 128), np.float32), XT[:, :S - 128]], axis=1))
        pmb_c = np.full((128, 1), (0.0 if odd else -1e30), np.float32)
        in_maps.append({
            "xt": xt_c,
            "wq4": wq4, "wk4": wk4, "wv4": wv4, "wo4": wo4,
            "bq": bq_r, "bk": bk_r, "bv": bv_r, "bo": bo_r,
            "wg0": wg0, "wg1": wg1, "wl": wl_t,
            "pmb": pmb_c,
        })
    return in_maps


def _assemble(results):
    outf = np.empty((B, S, D), dtype=np.float32)
    for c in range(8):
        b = c // 2
        odd = c % 2
        oc = results[c]["out"]
        for i in range(8):
            t_true = 2 * i + odd
            outf[b, t_true * 128:(t_true + 1) * 128, :] = oc[:, i * 128:(i + 1) * 128].T
    return outf


_CACHED = {}


def kernel(**inputs):
    in_maps = _prep_inputs(**inputs)
    if "nc" not in _CACHED:
        _CACHED["nc"], _ = build_program()
    res = run_bass_kernel_spmd(_CACHED["nc"], in_maps, core_ids=list(range(8)))
    return _assemble(res.results)


# version 06
# speedup vs baseline: 1.2305x; 1.0340x over previous
import numpy as np
from contextlib import ExitStack

import concourse.bass as bass
import concourse.mybir as mybir
from concourse.bass_utils import run_bass_kernel_spmd

F32 = mybir.dt.float32
BF16 = mybir.dt.bfloat16
FP8 = mybir.dt.float8e4
AF = mybir.ActivationFunctionType
ALU = mybir.AluOpType
AX = mybir.AxisListType

B, S, D, H = 4, 2048, 2048, 16
HID = 8192
NC = 16
NH = 64
NT = 16
NOWN = 1024
S2 = float(1.0 / np.sqrt(128.0))
EPS = 1e-5

ENGINES = ("T", "V", "S", "SP", "G")


class Op:
    __slots__ = ("eng", "emit", "deps", "inc", "sem_key", "needs_inc", "sem_val")

    def __init__(self, eng, emit, deps, inc, sem_key):
        self.eng = eng
        self.emit = emit
        self.deps = [d for d in deps if d is not None]
        self.inc = inc
        self.sem_key = sem_key
        self.needs_inc = False
        self.sem_val = None


class Prog:
    def __init__(self):
        self.eng_ops = {e: [] for e in ENGINES}
        self.last_dma = {}

    def op(self, eng, emit, deps=(), inc=1, sem_key=None):
        o = Op(eng, emit, deps, inc, sem_key or eng)
        if inc == 16:
            prev = self.last_dma.get(o.sem_key)
            if prev is not None:
                o.deps.append(prev)
            self.last_dma[o.sem_key] = o
        self.eng_ops[eng].append(o)
        return o

    def dma(self, eng, emit, deps=(), sem_key=None):
        assert sem_key is not None
        return self.op(eng, emit, deps, inc=16, sem_key=sem_key)

    def lower(self):
        for e in ENGINES:
            for o in self.eng_ops[e]:
                for d in o.deps:
                    if d.eng != o.eng or d.inc == 16:
                        d.needs_inc = True
        counts = {}
        keys = set()
        for e in ENGINES:
            for o in self.eng_ops[e]:
                if o.inc == 16:
                    o.needs_inc = True
                if o.needs_inc:
                    counts[o.sem_key] = counts.get(o.sem_key, 0) + o.inc
                    o.sem_val = counts[o.sem_key]
                    keys.add(o.sem_key)
        self.final_counts = counts
        return keys

    def emit_engine(self, eng_name, handle, sems):
        wm = {}
        for o in self.eng_ops[eng_name]:
            for d in o.deps:
                if d.eng == o.eng and d.inc != 16:
                    continue
                assert d.sem_val is not None, f"dep missing inc {d.eng}/{d.sem_key}"
                if wm.get(d.sem_key, -1) >= d.sem_val:
                    continue
                handle.wait_ge(sems[d.sem_key], d.sem_val)
                wm[d.sem_key] = d.sem_val
            inst = o.emit(handle)
            if o.needs_inc:
                if inst is None:
                    inst = handle.engine_nop()
                inst.then_inc(sems[o.sem_key], o.inc)


class Res:

    def __init__(self, seed_readers=()):
        self.writer = None
        self.readers = [r for r in seed_readers if r is not None]

    def write_deps(self):
        return list(self.readers) if self.readers else ([self.writer] if self.writer else [])

    def wrote(self, op):
        self.writer = op
        self.readers = []

    def touch(self, op):
        self.writer = op

    def read(self, op):
        if op is not None:
            self.readers.append(op)
        return op


def ts(i, n=128):
    return slice(i * n, (i + 1) * n)


def build_program(n_heads=H, stage="full", dbg_names=()):
    nc = bass.Bass("TRN2", target_bir_lowering=False)

    xt = nc.dram_tensor("xt", [D, S], F32, kind="ExternalInput")
    wq4 = nc.dram_tensor("wq4", [H, 128, D], F32, kind="ExternalInput")
    wk4 = nc.dram_tensor("wk4", [H, 128, D], F32, kind="ExternalInput")
    wv4 = nc.dram_tensor("wv4", [H, 128, D], F32, kind="ExternalInput")
    wo4 = nc.dram_tensor("wo4", [NC, 128, D], F32, kind="ExternalInput")
    bq = nc.dram_tensor("bq", [128, H], F32, kind="ExternalInput")
    bk = nc.dram_tensor("bk", [128, H], F32, kind="ExternalInput")
    bv = nc.dram_tensor("bv", [128, H], F32, kind="ExternalInput")
    bo = nc.dram_tensor("bo", [1, D], F32, kind="ExternalInput")
    wg0 = nc.dram_tensor("wg0", [NH, 128, D], F32, kind="ExternalInput")
    wg1 = nc.dram_tensor("wg1", [NH, 128, D], F32, kind="ExternalInput")
    wl = nc.dram_tensor("wl", [NH, 128, D], F32, kind="ExternalInput")
    pmb = nc.dram_tensor("pmb", [128, 1], F32, kind="ExternalInput")
    out = nc.dram_tensor("out", [D, NOWN], F32, kind="ExternalOutput")
    wl8 = nc.dram_tensor("wl8", [NH, 128, D], FP8)
    wg8 = nc.dram_tensor("wg8", [NH, 2, 128, D], FP8)

    dbg = {}

    def dbg_t(name, shape, dtype):
        if name in dbg_names:
            dbg[name] = nc.dram_tensor("dbg_" + name, shape, dtype, kind="ExternalOutput")
        return dbg.get(name)

    p = Prog()
    est = ExitStack()
    sb = lambda name, shape, dtype, **kw: est.enter_context(nc.sbuf_tensor(name, shape, dtype, **kw))
    dbg_ops = []

    def dump(eng, dst, src_ap, deps):
        if dst is None:
            return None
        o = p.dma(eng, lambda q, a=dst, b=src_ap: q.dma_start(a[:, :], b), deps=deps, sem_key="dbg")
        dbg_ops.append(o)
        return o

    ident = sb("ident", [128, 128], BF16)
    ones_sq = sb("ones_sq", [128, 128], BF16)
    tri = sb("tri", [128, 128], BF16)
    ones_r = sb("ones_r", [1, 128], BF16)
    ones_c = sb("ones_c", [128, 1], BF16)
    ones_row = sb("ones_row", [1, 512], BF16)
    zcol = sb("zcol", [128, 1], F32)
    pmb_sb = sb("pmb_sb", [128, 1], F32)
    bq_sb = sb("bq_sb", [128, H], F32)
    bk_sb = sb("bk_sb", [128, H], F32)
    bv_sb = sb("bv_sb", [128, H], F32)
    bo_sb = sb("bo_sb", [1, D], BF16)

    boot = []
    boot.append(p.op("G", lambda g: g.memset(ident[:], 0.0)))
    boot.append(p.op("G", lambda g: g.affine_select(
        out=ident[:], in_=ident[:], compare_op=ALU.not_equal, fill=1.0,
        base=0, pattern=[[-1, 128]], channel_multiplier=1)))
    boot.append(p.op("G", lambda g: g.memset(tri[:], 1.0)))
    boot.append(p.op("G", lambda g: g.affine_select(
        out=tri[:], in_=tri[:], compare_op=ALU.is_ge, fill=0.0,
        base=0, pattern=[[1, 128]], channel_multiplier=-1)))
    boot.append(p.op("G", lambda g: g.memset(ones_sq[:], 1.0)))
    boot.append(p.op("G", lambda g: g.memset(ones_r[:], 1.0)))
    boot.append(p.op("G", lambda g: g.memset(ones_c[:], 1.0)))
    boot.append(p.op("G", lambda g: g.memset(ones_row[:], 1.0)))
    boot.append(p.op("G", lambda g: g.memset(zcol[:], 0.0)))
    lnc = sb("lnc", [1, 8], F32)
    boot.append(p.op("G", lambda g: g.memset(lnc[:], float(0.5 * np.log(25.0 / (128.0 * S))))))
    bias_ld = []
    bias_ld.append(p.dma("SP", lambda q: q.dma_start(pmb_sb[:], pmb[:, :]), sem_key="bias"))
    bias_ld.append(p.dma("SP", lambda q: q.dma_start(bq_sb[:], bq[:, :]), sem_key="bias"))
    bias_ld.append(p.dma("SP", lambda q: q.dma_start(bk_sb[:], bk[:, :]), sem_key="bias"))
    bias_ld.append(p.dma("SP", lambda q: q.dma_start(bv_sb[:], bv[:, :]), sem_key="bias"))
    bias_ld.append(p.dma("G", lambda q: q.dma_start(bo_sb[:], bo[:, :]), sem_key="biasg"))
    boot_all = boot + bias_ld

    oall_scope = ExitStack()
    o_all = oall_scope.enter_context(nc.sbuf_tensor("o_all", [128, H * NOWN], BF16))
    xn_scope = ExitStack()
    xn = xn_scope.enter_context(nc.sbuf_tensor("xn", [128, NC * S], BF16))

    LW = 256
    NLC = S // LW
    ln_scope = ExitStack()
    xf32 = [ln_scope.enter_context(nc.sbuf_tensor(f"xf32_{i}", [128, NC * LW], F32)) for i in range(2)]
    xbf = [ln_scope.enter_context(nc.sbuf_tensor(f"xbf_{i}", [128, NC * LW], BF16)) for i in range(2)]
    sqbuf1 = ln_scope.enter_context(nc.sbuf_tensor("sqbuf1", [128, 2 * LW], BF16))
    lnb_m = ln_scope.enter_context(nc.sbuf_tensor("lnb_m", [128, LW], F32))
    lnb_a = ln_scope.enter_context(nc.sbuf_tensor("lnb_a", [128, LW], F32))
    lnb_b = ln_scope.enter_context(nc.sbuf_tensor("lnb_b", [128, LW], F32))
    tmpbuf = ln_scope.enter_context(nc.sbuf_tensor("tmpbuf", [128, 2 * LW], F32))

    lnp_scope = ExitStack()
    _ps_sum_full = lnp_scope.enter_context(nc.psum_tensor("ps_sum", [128, 512], F32))
    _ps_sq_full = lnp_scope.enter_context(nc.psum_tensor("ps_sq", [128, 512], F32))
    ps_sum = _ps_sum_full[:, 0:LW]
    ps_sq = _ps_sq_full[:, 0:LW]

    xslot = [Res(), Res()]
    xbslot = [Res(), Res()]
    ld_ops = {}
    ldb_ops = {}

    def x_ap(tc):
        return bass.AP(xt, tc * LW, [[S, 128], [128 * S, NC], [1, LW]])

    def get_ldx(tc):
        if tc in ld_ops:
            return ld_ops[tc]
        par = tc % 2
        ldx = p.dma("SP", lambda q, a=xf32[par], b=x_ap(tc): q.dma_start(a[:, :], b),
                    deps=xslot[par].write_deps(), sem_key=f"x{par}")
        xslot[par].wrote(ldx)
        ld_ops[tc] = ldx
        return ldx

    def get_ldb(tc):
        if tc in ldb_ops:
            return ldb_ops[tc]
        par = tc % 2
        ldb = p.dma("G", lambda q, a=xbf[par], b=x_ap(tc): q.dma_start(a[:, :], b),
                    deps=xbslot[par].write_deps() + (boot[:8] if tc == 0 else []),
                    sem_key=f"xb{par}")
        xbslot[par].wrote(ldb)
        ldb_ops[tc] = ldb
        return ldb

    ln1_fine = []
    r_sum1, r_sq1 = Res(), Res()
    sq_slot1 = [Res(), Res()]
    tmp_slot1 = [Res(), Res()]
    r_lnb = Res()

    get_ldx(0)
    get_ldb(0)
    get_ldx(1)
    get_ldb(1)
    for tc in range(NLC):
        par = tc % 2
        ldx = get_ldx(tc)
        ldb = get_ldb(tc)
        gf = lambda di, par=par: xf32[par][:, ts(di, LW)]
        gb = lambda di, par=par: xbf[par][:, ts(di, LW)]
        sum_mm = []
        sq_mm = []
        for di in range(NC):
            sl = di % 2
            sq_op = p.op("S", lambda s, di=di, sl=sl, gb=gb: s.activation(
                sqbuf1[:, ts(sl, LW)], gb(di), AF.Square),
                deps=[ldb] + sq_slot1[sl].write_deps())
            sq_slot1[sl].wrote(sq_op)
            xbslot[par].read(sq_op)
            m1 = p.op("T", lambda t, di=di, gb=gb: t.matmul(
                ps_sum[:, :], ones_sq[:], gb(di),
                start=(di == 0), stop=(di == NC - 1)),
                deps=([ldb] + r_sum1.write_deps()) if di == 0 else [])
            xbslot[par].read(m1)
            m2 = p.op("T", lambda t, di=di, sl=sl: t.matmul(
                ps_sq[:, :], ones_sq[:], sqbuf1[:, ts(sl, LW)],
                start=(di == 0), stop=(di == NC - 1)),
                deps=[sq_op] + (r_sq1.write_deps() if di == 0 else []))
            sq_slot1[sl].read(m2)
            sum_mm.append(m1)
            sq_mm.append(m2)
        r_sum1.wrote(sum_mm[-1])
        r_sq1.wrote(sq_mm[-1])
        f1 = p.op("V", lambda v: v.tensor_scalar_mul(lnb_m[:, :], ps_sum[:, :], 1.0 / D),
                  deps=[sum_mm[-1]] + r_lnb.write_deps())
        r_sum1.read(f1)
        f2 = p.op("V", lambda v: v.tensor_scalar(lnb_a[:, :], ps_sq[:, :], 1.0 / D, EPS,
                                                 ALU.mult, ALU.add),
                  deps=[sq_mm[-1]] + r_lnb.write_deps())
        r_sq1.read(f2)
        f4 = p.op("V", lambda v: v.tensor_tensor(lnb_b[:, :], lnb_m[:, :], lnb_m[:, :],
                                                 ALU.mult), deps=[f1, f2])
        f5 = p.op("V", lambda v: v.tensor_sub(lnb_a[:, :], lnb_a[:, :], lnb_b[:, :]),
                  deps=[f4])
        f6 = p.op("V", lambda v: v.reciprocal(lnb_b[:, :], lnb_a[:, :]), deps=[f5])
        f7 = p.op("S", lambda s: s.activation(lnb_a[:, :], lnb_b[:, :], AF.Sqrt), deps=[f6])
        last = None
        for di in range(NC):
            sl = di % 2
            u1 = p.op("V", lambda v, di=di, sl=sl, gf=gf: v.tensor_sub(
                tmpbuf[:, ts(sl, LW)], gf(di), lnb_m[:, :]),
                deps=[f1, ldx, f7] + tmp_slot1[sl].write_deps())
            tmp_slot1[sl].wrote(u1)
            xslot[par].read(u1)
            u2 = p.op("V", lambda v, di=di, sl=sl, tc=tc: v.tensor_tensor(
                xn[:, di * S + tc * LW: di * S + (tc + 1) * LW],
                tmpbuf[:, ts(sl, LW)], lnb_a[:, :], ALU.mult),
                deps=[u1])
            tmp_slot1[sl].read(u2)
            last = u2
        r_lnb.wrote(last)
        ln1_fine.append(last)
        if tc + 2 < NLC:
            get_ldx(tc + 2)
            get_ldb(tc + 2)

    ln1_chunks = [ln1_fine[2 * i + 1] for i in range(4)]
    ln1_psum_last = (r_sum1.readers[-1:] + r_sq1.readers[-1:])
    lnp_scope.close()
    ln_scope.close()

    d_xn = dbg_t("xn", [128, NC * S], BF16)
    dump("SP", d_xn, xn[:, :], ln1_chunks)

    if stage == "ln1":
        xn_scope.close()
        oall_scope.close()
        est.close()
        return _finish(nc, p, dbg_ops, None), dbg

    att_scope = ExitStack()
    asb = lambda name, shape, dtype: att_scope.enter_context(nc.sbuf_tensor(name, shape, dtype))
    wqb = [asb(f"wqb{i}", [128, D], BF16) for i in range(2)]
    wkb = [asb(f"wkb{i}", [128, D], BF16) for i in range(2)]
    wvb = [asb(f"wvb{i}", [128, D], BF16) for i in range(2)]
    k_h = [asb(f"k_h{i}", [128, S], BF16) for i in range(2)]
    v_f = [asb(f"v_f{i}", [128, S], BF16) for i in range(2)]
    v_t = [asb(f"v_t{i}", [128, S], BF16) for i in range(2)]
    q_h = [asb(f"q_h{i}", [128, NOWN], BF16) for i in range(2)]
    pt_b = asb("pt", [128, NT * NOWN], BF16)
    sqb = asb("sqb", [128, NOWN], BF16)
    nrm = asb("nrm", [1, NOWN + 16], F32)
    mtmp = asb("mtmp", [1, NOWN], F32)
    mrow = [asb(f"mrow{i}", [1, NOWN], BF16) for i in range(2)]
    invb = asb("invb", [128, 512], F32)
    wstg = [asb(f"wstg{i}", [128, D], FP8) for i in range(2)]

    attp_scope = ExitStack()
    pR = attp_scope.enter_context(nc.psum_tensor("pR", [128, NOWN], F32))
    pP = [attp_scope.enter_context(nc.psum_tensor(f"pP{i}", [128, 512], F32)) for i in range(4)]
    rp = [Res(ln1_psum_last) for _ in range(4)]
    rR = Res(ln1_psum_last)
    rB67 = [Res(ln1_psum_last), Res(ln1_psum_last)]

    wslot = {k: [Res(), Res()] for k in ("wq", "wk", "wv")}
    rk_s = [Res(), Res()]
    rv_s = [Res(), Res()]
    rvt_s = [Res(), Res()]
    rq_s = [Res(), Res()]
    r_sqb = Res()
    r_mtmp = Res()
    r_nrm = Res()
    r_mrow_s = [Res(), Res()]
    r_invb = Res()
    r_pt = [Res() for _ in range(NT)]
    r_oall = Res()

    wbuf = {"wq": wqb, "wk": wkb, "wv": wvb}
    ln_guard = [ln1_fine[-1]]

    def load_w(kind, src, h):
        par = h % 2
        slot = wslot[kind][par]
        o = p.dma("G", lambda q, src=src, h=h, par=par, kind=kind: q.dma_start(
            wbuf[kind][par][:, :], src[h]),
            deps=slot.write_deps() + (boot_all + ln_guard if h <= 1 else []),
            sem_key=f"{kind}{par}")
        slot.wrote(o)
        return o

    wload = {}
    for h0 in range(min(2, n_heads)):
        for kind, src in (("wq", wq4), ("wk", wk4), ("wv", wv4)):
            wload[(kind, h0)] = load_w(kind, src, h0)

    pre_jobs = []
    for m in range(NH):
        pre_jobs.append((wg0[m], bass.AP(wg8, (m * 2 + 0) * 128 * D, [[D, 128], [1, D]])))
        pre_jobs.append((wg1[m], bass.AP(wg8, (m * 2 + 1) * 128 * D, [[D, 128], [1, D]])))
        pre_jobs.append((wl[m], wl8[m]))
    wg8_done = [[None, None] for _ in range(NH)]
    wl8_done = [None] * NH
    wstg_slot = [Res(), Res()]
    _pre_i = [0]

    def emit_prepass(njobs):
        for _ in range(njobs):
            i = _pre_i[0]
            if i >= len(pre_jobs):
                return
            _pre_i[0] += 1
            src, dst = pre_jobs[i]
            sl = i % 2
            ld = p.dma("G", lambda q, src=src, sl=sl: q.dma_start(wstg[sl][:, :], src),
                       deps=wstg_slot[sl].write_deps() + (boot_all + ln_guard if i <= 1 else []),
                       sem_key=f"pre{sl}")
            st = p.dma("SP", lambda q, dst=dst, sl=sl: q.dma_start(dst, wstg[sl][:, :]),
                       deps=[ld], sem_key=f"pres{sl}")
            wstg_slot[sl].wrote(st)
            kind = i % 3
            m = i // 3
            if kind == 0:
                wg8_done[m][0] = st
            elif kind == 1:
                wg8_done[m][1] = st
            else:
                wl8_done[m] = st

    qstart = [128 * (kc // 2) for kc in range(NT)]
    head_out_ops = []
    d_k0 = dbg_t("k0", [128, S], BF16)
    d_vt0 = dbg_t("vt0", [128, S], BF16)
    d_q0 = dbg_t("q0", [128, NOWN], BF16)
    d_mrow0 = dbg_t("mrow0", [1, NOWN], BF16)
    d_nrm0 = dbg_t("nrm0", [1, NOWN + 16], F32)
    d_pt0 = dbg_t("pt0", [128, NT * NOWN], BF16)

    def emit_proj_bounds(h):
        par = h % 2
        kb, vb, qb = k_h[par], v_f[par], q_h[par]
        rk, rv, rq = rk_s[par], rv_s[par], rq_s[par]
        r_mrow = r_mrow_s[par]

        k_ev = []
        k_mm_last = None
        for tc in range(4):
            bank = tc % 2
            mms = []
            for di in range(NC):
                mm = p.op("T", lambda t, di=di, tc=tc, bank=bank, par=par: t.matmul(
                    pP[bank][:, :], wkb[par][:, ts(di)],
                    xn[:, di * S + tc * 512: di * S + (tc + 1) * 512],
                    start=(di == 0), stop=(di == NC - 1)),
                    deps=([wload[("wk", h)], ln1_chunks[tc]] + rp[bank].write_deps()
                          + (rk.write_deps() if tc < 2 else [])) if di == 0 else [])
                mms.append(mm)
            rp[bank].wrote(mms[-1])
            k_mm_last = mms[-1]
            ev = p.op("S", lambda s, tc=tc, bank=bank, kb=kb, h=h: s.activation(
                kb[:, tc * 512:(tc + 1) * 512], pP[bank][:, :], AF.Identity,
                bias=bk_sb[:, h:h + 1]),
                deps=[mms[-1]] + (bias_ld if h == 0 and tc == 0 else []))
            rp[bank].read(ev)
            k_ev.append(ev)
        rk.wrote(k_ev[-1])
        wslot["wk"][par].read(k_mm_last)

        v_ev = []
        v_mm_last = None
        for tc in range(4):
            bank = 2 + tc % 2
            mms = []
            for di in range(NC):
                mm = p.op("T", lambda t, di=di, tc=tc, bank=bank, par=par: t.matmul(
                    pP[bank][:, :], wvb[par][:, ts(di)],
                    xn[:, di * S + tc * 512: di * S + (tc + 1) * 512],
                    start=(di == 0), stop=(di == NC - 1)),
                    deps=([wload[("wv", h)], ln1_chunks[tc]] + rp[bank].write_deps()
                          + (rv.write_deps() if tc < 2 else [])) if di == 0 else [])
                mms.append(mm)
            rp[bank].wrote(mms[-1])
            v_mm_last = mms[-1]
            ev = p.op("S", lambda s, tc=tc, bank=bank, vb=vb, h=h: s.activation(
                vb[:, tc * 512:(tc + 1) * 512], pP[bank][:, :], AF.Identity,
                bias=bv_sb[:, h:h + 1]),
                deps=[mms[-1]] + (bias_ld if h == 0 and tc == 0 else []))
            rp[bank].read(ev)
            v_ev.append(ev)
        rv.wrote(v_ev[-1])
        wslot["wv"][par].read(v_mm_last)

        q_ev = []
        q_mm_last = None
        for half in range(2):
            bank = half
            mms = []
            for di in range(NC):
                rhs = bass.AP(xn, di * S + 128 + half * 1024, [[NC * S, 128], [256, 4], [1, 128]])
                mm = p.op("T", lambda t, di=di, bank=bank, rhs=rhs, par=par: t.matmul(
                    pP[bank][:, :], wqb[par][:, ts(di)], rhs,
                    start=(di == 0), stop=(di == NC - 1)),
                    deps=([wload[("wq", h)], ln1_chunks[3]] + rp[bank].write_deps()
                          + (rq.write_deps() if half == 0 else [])) if di == 0 else [])
                mms.append(mm)
            rp[bank].wrote(mms[-1])
            q_mm_last = mms[-1]
            ev = p.op("S", lambda s, half=half, bank=bank, qb=qb, h=h: s.activation(
                qb[:, half * 512:(half + 1) * 512], pP[bank][:, :], AF.Identity,
                bias=bq_sb[:, h:h + 1]),
                deps=[mms[-1]] + (bias_ld if h == 0 and half == 0 else []))
            rp[bank].read(ev)
            q_ev.append(ev)
        rq.wrote(q_ev[-1])
        wslot["wq"][par].read(q_mm_last)

        if h + 2 < n_heads:
            for kind, src in (("wq", wq4), ("wk", wk4), ("wv", wv4)):
                wload[(kind, h + 2)] = load_w(kind, src, h + 2)

        kn_acc = []
        ksq_mm = []
        for tc in range(4):
            sl = tc % 2
            ksq = p.op("V", lambda v, kb=kb, tc=tc, sl=sl: v.tensor_tensor(
                sqb[:, ts(sl, 512)], kb[:, tc * 512:(tc + 1) * 512],
                kb[:, tc * 512:(tc + 1) * 512], ALU.mult),
                deps=[k_ev[tc]] + ([ksq_mm[tc - 2]] if tc >= 2 else r_sqb.write_deps()))
            rk.read(ksq)
            bank = 2 + tc % 2
            mm = p.op("T", lambda t, sl=sl, bank=bank: t.matmul(
                pP[bank][:, :], ones_sq[:], sqb[:, ts(sl, 512)],
                start=True, stop=True),
                deps=[ksq] + rp[bank].write_deps())
            rp[bank].wrote(mm)
            ksq_mm.append(mm)
            if tc == 0:
                ev = p.op("V", lambda v, bank=bank: v.tensor_copy(
                    mtmp[:, 0:512], pP[bank][0:1, :]),
                    deps=[mm] + r_mtmp.write_deps())
            else:
                ev = p.op("V", lambda v, bank=bank: v.tensor_tensor(
                    mtmp[:, 0:512], mtmp[:, 0:512], pP[bank][0:1, :], ALU.add),
                    deps=[mm, kn_acc[-1]])
            rp[bank].read(ev)
            kn_acc.append(ev)
        r_mtmp.wrote(kn_acc[-1])
        ksum = p.op("V", lambda v: v.reduce_sum(nrm[:, NOWN + 9:NOWN + 10],
                                                mtmp[:, 0:512], axis=AX.X),
                    deps=[kn_acc[-1]] + r_nrm.write_deps())
        r_mtmp.read(ksum)
        qsq = p.op("V", lambda v, qb=qb: v.tensor_tensor(sqb[:, 0:NOWN], qb[:, :], qb[:, :],
                                                         ALU.mult),
                   deps=[q_ev[-1], ksq_mm[-1], ksq_mm[-2]])
        rq.read(qsq)
        r_sqb.wrote(qsq)
        qn_ev = []
        for half in range(2):
            bank = half
            mm = p.op("T", lambda t, half=half, bank=bank: t.matmul(
                pP[bank][:, :], ones_sq[:], sqb[:, half * 512:(half + 1) * 512],
                start=True, stop=True),
                deps=[qsq] + rp[bank].write_deps())
            rp[bank].wrote(mm)
            r_sqb.read(mm)
            ev = p.op("V", lambda v, half=half, bank=bank: v.tensor_copy(
                nrm[:, half * 512:(half + 1) * 512], pP[bank][0:1, :]),
                deps=[mm] + (r_nrm.write_deps() if half == 0 else []))
            rp[bank].read(ev)
            qn_ev.append(ev)
        t1 = p.op("S", lambda s: s.activation(mtmp[:, :], nrm[:, 0:NOWN], AF.Ln,
                                              scale=nrm[:, NOWN + 9:NOWN + 10]),
                  deps=qn_ev + [ksum] + r_mtmp.write_deps())
        r_mtmp.wrote(t1)
        t2 = p.op("S", lambda s: s.activation(nrm[:, 0:NOWN], mtmp[:, :], AF.Exp,
                                              scale=0.5, bias=lnc[:, 0:1]), deps=[t1])
        r_mtmp.read(t2)
        mr = p.op("V", lambda v, par=par: v.tensor_scalar_mul(mrow[par][:, :], nrm[:, 0:NOWN], -1.0),
                  deps=[t2] + r_mrow.write_deps())
        r_mrow.wrote(mr)
        r_nrm.wrote(t2)
        r_nrm.read(mr)
        emit_prepass(12)
        return dict(k_ev=k_ev, v_ev=v_ev, q_ev=q_ev, mr=mr,
                    proj_done=[k_mm_last, v_mm_last, q_mm_last])

    def emit_attn_core(h, pb):
        par = h % 2
        kb, vb, vt, qb = k_h[par], v_f[par], v_t[par], q_h[par]
        rk, rv, rvt, rq = rk_s[par], rv_s[par], rvt_s[par], rq_s[par]
        r_mrow = r_mrow_s[par]
        k_ev, v_ev, q_ev, mr = pb["k_ev"], pb["v_ev"], pb["q_ev"], pb["mr"]

        with nc.psum_tensor(f"pT0_{h}", [128, 1024], BF16) as pT0, \
             nc.psum_tensor(f"pT1_{h}", [128, 1024], BF16) as pT1:
            pTs = [pT0, pT1]
            tr_ev = []
            for kc in range(NT):
                bk_ = kc % 2
                trm = p.op("T", lambda t, kc=kc, bk_=bk_, vb=vb, pTs=pTs: t.transpose(
                    pTs[bk_][:, 0:128], vb[:, ts(kc)], ident[:]),
                    deps=[v_ev[kc // 4]] + rB67[bk_].write_deps())
                rB67[bk_].wrote(trm)
                rv.read(trm)
                ev = p.op("V", lambda v, kc=kc, bk_=bk_, vt=vt, pTs=pTs: v.tensor_copy(
                    vt[:, ts(kc)], pTs[bk_][:, 0:128]),
                    deps=[trm] + (rvt.write_deps() if kc < 2 else []))
                rB67[bk_].read(ev)
                tr_ev.append(ev)
            rvt.wrote(tr_ev[-1])

        with nc.psum_tensor(f"pO0_{h}", [128, 512], F32) as pO0, \
             nc.psum_tensor(f"pO1_{h}", [128, 512], F32) as pO1:
            pOs = [pO0, pO1]
            exp_done = {}
            pv_ops = []
            r_ops = []

            def emit_st(kc, kb=kb, qb=qb, par=par):
                qs = qstart[kc]
                pair = (0, 1) if (kc % 2 == 0) else (2, 3)
                segs = []
                for half in range(2):
                    lo = max(qs, half * 512)
                    hi = (half + 1) * 512
                    if lo >= hi:
                        continue
                    w_ = hi - lo
                    bank = pair[half]
                    mm1 = p.op("T", lambda t, kc=kc, lo=lo, w_=w_, bank=bank: t.matmul(
                        pP[bank][:, 0:w_], kb[:, ts(kc)], qb[:, lo:lo + w_],
                        start=True, stop=False),
                        deps=[k_ev[kc // 4], q_ev[lo // 512]] + rp[bank].write_deps())
                    mm2 = p.op("T", lambda t, kc=kc, lo=lo, w_=w_, bank=bank: t.matmul(
                        pP[bank][:, 0:w_], ones_r[:], mrow[par][:, lo:lo + w_],
                        start=False, stop=True), deps=[mr])
                    rp[bank].wrote(mm2)
                    rk.read(mm1)
                    rq.read(mm1)
                    r_mrow.read(mm2)
                    bias_ap = pmb_sb[:, 0:1] if kc == 0 else zcol[:, 0:1]
                    evx = p.op("S", lambda s, kc=kc, lo=lo, w_=w_, bank=bank, bias_ap=bias_ap: s.activation(
                        pt_b[:, kc * NOWN + lo: kc * NOWN + lo + w_], pP[bank][:, 0:w_],
                        AF.Exp, bias=bias_ap, scale=S2),
                        deps=[mm2] + (r_pt[kc].write_deps() if lo == qs else []))
                    rp[bank].read(evx)
                    segs.append(evx)
                last = segs[-1]
                if kc % 2 == 1:
                    i0 = (kc - 1) // 2
                    dm = p.op("V", lambda v, kc=kc, i0=i0: v.tensor_tensor(
                        pt_b[:, kc * NOWN + i0 * 128: kc * NOWN + (i0 + 1) * 128],
                        pt_b[:, kc * NOWN + i0 * 128: kc * NOWN + (i0 + 1) * 128],
                        tri[:], ALU.mult), deps=segs)
                    last = dm
                r_pt[kc].wrote(last)
                exp_done[kc] = last

            def emit_pv(kc, vt=vt):
                qs = qstart[kc]
                last = exp_done[kc]
                for half in range(2):
                    lo = max(qs, half * 512)
                    hi = (half + 1) * 512
                    if lo >= hi:
                        continue
                    w_ = hi - lo
                    mmo = p.op("T", lambda t, kc=kc, lo=lo, w_=w_, half=half, pOs=pOs: t.matmul(
                        pOs[half][:, lo - half * 512: lo - half * 512 + w_],
                        vt[:, ts(kc)], pt_b[:, kc * NOWN + lo: kc * NOWN + lo + w_],
                        start=(kc == 0), stop=(kc == NT - 1)),
                        deps=[last, tr_ev[kc]] + (rB67[half].write_deps() if kc == 0 else []))
                    if kc == 0:
                        rB67[half].wrote(mmo)
                    else:
                        rB67[half].touch(mmo)
                    rvt.read(mmo)
                    r_pt[kc].read(mmo)
                    pv_ops.append(mmo)
                    mmr = p.op("T", lambda t, kc=kc, lo=lo, w_=w_: t.matmul(
                        pR[:, lo:lo + w_], ones_sq[:], pt_b[:, kc * NOWN + lo: kc * NOWN + lo + w_],
                        start=(kc == 0), stop=(kc == NT - 1)),
                        deps=[last] + (rR.write_deps() if kc == 0 else []))
                    if kc == 0:
                        rR.wrote(mmr)
                    else:
                        rR.touch(mmr)
                    r_pt[kc].read(mmr)
                    r_ops.append(mmr)

            emit_st(0)
            emit_st(1)
            emit_st(2)
            for kc in range(3, NT):
                emit_st(kc)
                emit_pv(kc - 3)
            emit_pv(NT - 3)
            emit_pv(NT - 2)
            emit_pv(NT - 1)

            o_evs = []
            for half in range(2):
                rcp = p.op("V", lambda v, half=half: v.reciprocal(
                    invb[:, :], pR[:, half * 512:(half + 1) * 512]),
                    deps=[r_ops[-1]] + r_invb.write_deps())
                rR.read(rcp)
                r_invb.wrote(rcp)
                oe = p.op("V", lambda v, half=half, h=h, pOs=pOs: v.tensor_tensor(
                    o_all[:, h * NOWN + half * 512: h * NOWN + (half + 1) * 512],
                    pOs[half][:, :], invb[:, :], ALU.mult),
                    deps=[rcp, pv_ops[-1], pv_ops[-2] if len(pv_ops) >= 2 else None])
                rB67[half].read(oe)
                r_invb.read(oe)
                o_evs.append(oe)
            head_out_ops.append(o_evs[-1])

        if h == 0:
            dump("SP", d_k0, k_h[0][:, :], [k_ev[-1]])
            rk.read(dump("SP", d_vt0, v_t[0][:, :], [tr_ev[-1]]))
            dump("SP", d_q0, q_h[0][:, :], [q_ev[-1]])
            dump("SP", d_mrow0, mrow[0][:, :], [mr])
            dump("SP", d_nrm0, nrm[:, :], [mr])
            if d_pt0 is not None:
                o = dump("SP", d_pt0, pt_b[:, :], [exp_done[kc] for kc in range(NT)])
                for kc in range(NT):
                    r_pt[kc].read(o)

    pb_next = emit_proj_bounds(0)
    for h in range(n_heads):
        pb_cur = pb_next
        if h + 1 < n_heads:
            pb_next = emit_proj_bounds(h + 1)
        emit_attn_core(h, pb_cur)
    emit_prepass(len(pre_jobs))

    att_last = head_out_ops[-1]
    last_proj_done = pb_cur["proj_done"]
    d_oall = dbg_t("o_all", [128, H * NOWN], BF16)
    do_ = dump("SP", d_oall, o_all[:, :], [att_last])
    r_oall.read(do_)

    attp_scope.close()
    att_scope.close()
    xn_scope.close()

    if stage == "attn":
        oall_scope.close()
        est.close()
        return _finish(nc, p, dbg_ops, None), dbg

    x1 = sb("x1", [128, NC * NOWN], F32, side="right")
    wo_scope = ExitStack()
    wob = [wo_scope.enter_context(nc.sbuf_tensor(f"wob{i}", [128, D], BF16)) for i in range(2)]
    x_own = wo_scope.enter_context(nc.sbuf_tensor("x_own", [128, NC * NOWN], F32))

    wop_scope = ExitStack()
    pW = [wop_scope.enter_context(nc.psum_tensor(f"pW{i}", [128, 512], F32)) for i in range(4)]
    rW = [Res([att_last]) for _ in range(4)]

    xo_guard = list(last_proj_done) + (dbg_ops[-1:] if dbg_ops else [])
    xo_l = []
    for di in range(NC):
        in_ap = bass.AP(xt, di * 128 * S + 128, [[S, 128], [256, 8], [1, 128]])
        xo_l.append(p.dma("SP", lambda q, di=di, b=in_ap: q.dma_start(
            x_own[:, di * NOWN:(di + 1) * NOWN], b), deps=xo_guard, sem_key=f"xo{di % 2}"))

    wo_slot = [Res(xo_guard), Res(xo_guard)]

    def load_wo(m):
        par = m % 2
        o = p.dma("G", lambda q, m=m, par=par: q.dma_start(wob[par][:, :], wo4[m]),
                  deps=wo_slot[par].write_deps(), sem_key=f"wo{par}")
        wo_slot[par].wrote(o)
        return o

    wo_l = {0: load_wo(0), 1: load_wo(1)}
    x1_ready = {}
    for m in range(NC):
        par = m % 2
        banks = (0, 1) if par == 0 else (2, 3)
        last_mm_b = None
        for half in range(2):
            bank = banks[half]
            mms = []
            for di in range(NC):
                mm = p.op("T", lambda t, m=m, di=di, half=half, bank=bank, par=par: t.matmul(
                    pW[bank][:, :], wob[par][:, ts(di)],
                    o_all[:, di * NOWN + half * 512: di * NOWN + (half + 1) * 512],
                    start=(di == 0), stop=False),
                    deps=([wo_l[m], att_last] + rW[bank].write_deps()) if di == 0 else [])
                mms.append(mm)
            if m < 2 and half == 0:
                r_oall.read(mms[0])
            mmb = p.op("T", lambda t, m=m, bank=bank: t.matmul(
                pW[bank][:, :], bo_sb[:, ts(m)], ones_row[:, :], start=False, stop=True),
                deps=[bias_ld[-1]] if m == 0 else [])
            rW[bank].wrote(mmb)
            wo_slot[par].read(mmb)
            last_mm_b = mmb
            ev = p.op("V", lambda v, m=m, half=half, bank=bank: v.tensor_tensor(
                x1[:, m * NOWN + half * 512: m * NOWN + (half + 1) * 512],
                pW[bank][:, :], x_own[:, m * NOWN + half * 512: m * NOWN + (half + 1) * 512],
                ALU.add), deps=[mmb, xo_l[m]])
            rW[bank].read(ev)
            x1_ready[(m, half)] = ev
        if m + 2 < NC:
            wo_l[m + 2] = load_wo(m + 2)

    x1_all = [x1_ready[(m, hf)] for m in range(NC) for hf in range(2)]
    d_x1 = dbg_t("x1", [128, NC * NOWN], F32)
    dump("SP", d_x1, x1[:, :], x1_all)

    wop_scope.close()
    wo_scope.close()
    oall_scope.close()
    if stage == "wo":
        est.close()
        return _finish(nc, p, dbg_ops, None), dbg

    xn2 = sb("xn2", [128, NC * NOWN], FP8)
    act_b = sb("act", [128, NH * NOWN], FP8)
    ffn_scope = ExitStack()
    fsb = lambda name, shape, dtype: ffn_scope.enter_context(nc.sbuf_tensor(name, shape, dtype))
    wgb = [fsb(f"wgb{i}", [128, 2 * D], FP8) for i in range(2)]
    wlb = [fsb(f"wlb{i}", [128, 4 * D], FP8) for i in range(2)]
    tgb = fsb("tgb", [128, 2 * 512], BF16)
    ostg = fsb("ostg", [128, 16 * 256], F32)
    l2sq = fsb("l2sq", [128, 2 * 512], BF16)
    l2bf = fsb("l2bf", [128, 2 * 512], BF16)
    l2b_m = fsb("l2b_m", [128, 512], F32)
    l2b_a = fsb("l2b_a", [128, 512], F32)
    l2tmp = fsb("l2tmp", [128, 2 * 512], F32)
    l2b_b = l2tmp[:, 0:512]

    ln2p_scope = ExitStack()
    p2_sum = ln2p_scope.enter_context(nc.psum_tensor("p2_sum", [128, 512], F32))
    p2_sq = ln2p_scope.enter_context(nc.psum_tensor("p2_sq", [128, 512], F32))
    wo_psum_last = [rW[i].readers[-1] for i in range(4) if rW[i].readers]
    r_sum2, r_sq2 = (Res(wo_psum_last) for _ in range(2))
    r_lnb2 = Res()
    sq_slot2 = [Res(), Res()]
    tmp_slot2 = [Res(), Res()]
    l2bf_slot = [Res(), Res()]

    xn2_done = []
    for tc in range(2):
        sum_mm = []
        sq_mm = []
        for di in range(NC):
            sl = di % 2
            src = l2bf[:, ts(sl, 512)]
            cp = p.op("S", lambda s, di=di, sl=sl, tc=tc: s.activation(
                l2bf[:, ts(sl, 512)],
                x1[:, di * NOWN + tc * 512: di * NOWN + (tc + 1) * 512], AF.Identity),
                deps=[x1_ready[(di, tc)]] + l2bf_slot[sl].write_deps())
            l2bf_slot[sl].wrote(cp)
            sq_op = p.op("V", lambda v, sl=sl, src=src: v.tensor_tensor(
                l2sq[:, ts(sl, 512)], src, src, ALU.mult),
                deps=[cp] + sq_slot2[sl].write_deps())
            sq_slot2[sl].wrote(sq_op)
            l2bf_slot[sl].read(sq_op)
            m1 = p.op("T", lambda t, di=di, src=src: t.matmul(
                p2_sum[:, :], ones_sq[:], src, start=(di == 0), stop=(di == NC - 1)),
                deps=([cp] + r_sum2.write_deps()) if di == 0 else [cp])
            m2 = p.op("T", lambda t, di=di, sl=sl: t.matmul(
                p2_sq[:, :], ones_sq[:], l2sq[:, ts(sl, 512)],
                start=(di == 0), stop=(di == NC - 1)),
                deps=[sq_op] + (r_sq2.write_deps() if di == 0 else []))
            sq_slot2[sl].read(m2)
            l2bf_slot[sl].read(m1)
            sum_mm.append(m1)
            sq_mm.append(m2)
        r_sum2.wrote(sum_mm[-1])
        r_sq2.wrote(sq_mm[-1])
        f1 = p.op("V", lambda v: v.tensor_scalar_mul(l2b_m[:, :], p2_sum[:, :], 1.0 / D),
                  deps=[sum_mm[-1]] + r_lnb2.write_deps())
        r_sum2.read(f1)
        f2 = p.op("V", lambda v: v.tensor_scalar(l2b_a[:, :], p2_sq[:, :], 1.0 / D, EPS,
                                                 ALU.mult, ALU.add),
                  deps=[sq_mm[-1]] + r_lnb2.write_deps())
        r_sq2.read(f2)
        f4 = p.op("V", lambda v: v.tensor_tensor(l2b_b, l2b_m[:, :], l2b_m[:, :],
                                                 ALU.mult), deps=[f1, f2])
        f5 = p.op("V", lambda v: v.tensor_sub(l2b_a[:, :], l2b_a[:, :], l2b_b),
                  deps=[f4])
        f6 = p.op("V", lambda v: v.reciprocal(l2b_b, l2b_a[:, :]), deps=[f5])
        f7 = p.op("S", lambda s: s.activation(l2b_a[:, :], l2b_b, AF.Sqrt), deps=[f6])
        last = None
        for di in range(NC):
            sl = di % 2
            u1 = p.op("V", lambda v, di=di, sl=sl, tc=tc: v.tensor_sub(
                l2tmp[:, ts(sl, 512)],
                x1[:, di * NOWN + tc * 512: di * NOWN + (tc + 1) * 512], l2b_m[:, :]),
                deps=[f1, f7] + tmp_slot2[sl].write_deps())
            tmp_slot2[sl].wrote(u1)
            u2 = p.op("V", lambda v, di=di, sl=sl, tc=tc: v.tensor_tensor(
                xn2[:, di * NOWN + tc * 512: di * NOWN + (tc + 1) * 512],
                l2tmp[:, ts(sl, 512)], l2b_a[:, :], ALU.mult),
                deps=[u1])
            tmp_slot2[sl].read(u2)
            last = u2
        r_lnb2.wrote(last)
        xn2_done.append(last)
    ln2_psum_last = (r_sum2.readers[-1:] + r_sq2.readers[-1:] + [xn2_done[-1]])
    ln2p_scope.close()
    d_xn2 = dbg_t("xn2", [128, NC * NOWN], F32)
    dump("G", d_xn2, xn2[:, :], xn2_done)

    mm1p_scope = ExitStack()
    pG = [mm1p_scope.enter_context(nc.psum_tensor(f"pG{i}", [128, 512], F32)) for i in range(2)]
    pF = [mm1p_scope.enter_context(nc.psum_tensor(f"pF{i}", [128, 512], F32)) for i in range(2)]
    rG = [Res(ln2_psum_last), Res(ln2_psum_last)]
    rF = [Res(ln2_psum_last), Res(ln2_psum_last)]

    ffn_guard = [x1_all[-1]] + (dbg_ops[-1:] if dbg_ops else [])
    wg_slot = [Res(ffn_guard), Res(ffn_guard)]

    def load_wg(m):
        par = m % 2
        in_ap = bass.AP(wg8, m * 2 * 128 * D, [[D, 128], [128 * D, 2], [1, D]])
        d = p.dma("SP", lambda q, b=in_ap, par=par: q.dma_start(wgb[par][:, :], b),
                  deps=wg_slot[par].write_deps() + wg8_done[m], sem_key=f"wg{par}a")
        wg_slot[par].wrote(d)
        return (d, d)

    wg_l = {0: load_wg(0), 1: load_wg(1)}
    act_ready = [None] * NH
    last_ge = None

    NDP = NC // 2

    def mm1_aps(par, gsel, dp, half):
        lhsT = bass.AP(wgb[par], gsel * D + 2 * dp * 128, [[2 * D, 128], [128, 2], [1, 128]])
        rhs = bass.AP(xn2, (2 * dp) * NOWN + half * 512, [[NC * NOWN, 128], [NOWN, 2], [1, 512]])
        return lhsT, rhs

    for m in range(NH):
        par = m % 2
        d1, d2 = wg_l[m]
        for half in range(2):
            mg = []
            for dp in range(NDP):
                mm = p.op("T", lambda t, dp=dp, half=half, par=par: t.matmul(
                    pG[par][:, :], *mm1_aps(par, 0, dp, half),
                    start=(dp == 0), stop=(dp == NDP - 1),
                    perf_mode=mybir.MatmulPerfMode.DoubleRow),
                    deps=([d1, xn2_done[-1]] + rG[par].write_deps()) if dp == 0 else [])
                mg.append(mm)
            rG[par].wrote(mg[-1])
            mf = []
            for dp in range(NDP):
                mm = p.op("T", lambda t, dp=dp, half=half, par=par: t.matmul(
                    pF[par][:, :], *mm1_aps(par, 1, dp, half),
                    start=(dp == 0), stop=(dp == NDP - 1),
                    perf_mode=mybir.MatmulPerfMode.DoubleRow),
                    deps=([d2] + rF[par].write_deps()) if dp == 0 else [])
                mf.append(mm)
            rF[par].wrote(mf[-1])
            ge = p.op("S", lambda s, half=half, par=par: s.activation(
                tgb[:, ts(par, 512)], pG[par][:, :], AF.Gelu_apprx_tanh), deps=[mg[-1]])
            rG[par].read(ge)
            last_ge = ge
            mu = p.op("V", lambda v, m=m, half=half, par=par: v.tensor_tensor(
                act_b[:, m * NOWN + half * 512: m * NOWN + (half + 1) * 512],
                tgb[:, ts(par, 512)], pF[par][:, :], ALU.mult), deps=[ge, mf[-1]])
            rF[par].read(mu)
            wg_slot[par].read(mu)
            act_ready[m] = mu
        if m + 2 < NH:
            wg_l[m + 2] = load_wg(m + 2)

    d_act = dbg_t("act", [128, NH * NOWN], F32)
    dump("G", d_act, act_b[:, :], [act_ready[NH - 1]])
    mm1_psum_last = [op for op in (last_ge, act_ready[NH - 1]) if op is not None]
    mm1p_scope.close()

    mm2p_scope = ExitStack()
    pM = mm2p_scope.enter_context(nc.psum_tensor("pM", [128, NC * 256], F32))
    rM = Res(mm1_psum_last)

    wl_slot = [Res(), Res()]
    r_ostg = [Res(), Res()]
    out_sts = []

    def load_wl8(qt, blk):
        par = blk % 2
        in_ap = bass.AP(wl8, blk * 4 * 128 * D, [[D, 128], [128 * D, 4], [1, D]])
        o = p.dma("G", lambda q, b=in_ap, par=par: q.dma_start(wlb[par][:, :], b),
                  deps=(wl_slot[par].write_deps() + ffn_guard
                        + [wl8_done[blk * 4 + j] for j in range(4)]),
                  sem_key=f"wlq{par}")
        wl_slot[par].wrote(o)
        return o

    pending = {(0, 0): load_wl8(0, 0), (0, 1): load_wl8(0, 1)}
    NBLK = 16
    for qt in range(4):
        quarter_last_mm = None
        for blk in range(NBLK):
            par = blk % 2
            ld = pending.pop((qt, blk))
            blk_last = None
            for hi in (0, 2):
                hh = blk * 4 + hi
                for f in range(NC):
                    st_flag = (hh == 0 and f % 2 == 0)
                    lhsT = bass.AP(wlb[par], hi * D + f * 128, [[4 * D, 128], [D, 2], [1, 128]])
                    rhs = bass.AP(act_b, hh * NOWN + qt * 256, [[NH * NOWN, 128], [NOWN, 2], [1, 256]])
                    mm = p.op("T", lambda t, f=f, st_flag=st_flag, hh=hh, lhsT=lhsT, rhs=rhs: t.matmul(
                        pM[:, f * 256:(f + 1) * 256], lhsT, rhs,
                        start=st_flag, stop=(hh == NH - 2),
                        perf_mode=mybir.MatmulPerfMode.DoubleRow),
                        deps=([ld, act_ready[hh + 1]] + (rM.write_deps() if hh == 0 else [])) if f == 0 else [])
                    blk_last = mm
            wl_slot[par].read(blk_last)
            nxt = (qt, blk + 2) if blk + 2 < NBLK else ((qt + 1, blk + 2 - NBLK) if qt < 3 else None)
            if nxt is not None and nxt not in pending:
                pending[nxt] = load_wl8(*nxt)
            quarter_last_mm = blk_last
        rM.wrote(quarter_last_mm)
        evs = []
        for fg in range(2):
            for f8 in range(8):
                f = fg * 8 + f8
                ev = p.op("V", lambda v, qt=qt, f=f, f8=f8, fg=fg: v.tensor_tensor(
                    ostg[:, fg * 2048 + f8 * 256: fg * 2048 + (f8 + 1) * 256],
                    pM[:, f * 256:(f + 1) * 256],
                    x1[:, f * NOWN + qt * 256: f * NOWN + qt * 256 + 256], ALU.add),
                    deps=[quarter_last_mm] + (r_ostg[fg].write_deps() if f8 == 0 else []))
                rM.read(ev)
                evs.append(ev)
            out_ap = bass.AP(out, fg * 8 * 128 * NOWN + qt * 256,
                             [[NOWN, 128], [128 * NOWN, 8], [1, 256]])
            st = p.dma("SP", lambda q, a=out_ap, fg=fg: q.dma_start(
                a, ostg[:, fg * 2048:(fg + 1) * 2048]),
                deps=evs[-8:], sem_key="out")
            r_ostg[fg].wrote(evs[-1])
            r_ostg[fg].read(st)
            out_sts.append(st)

    mm2p_scope.close()
    ffn_scope.close()
    est.close()
    return _finish(nc, p, dbg_ops, out_sts), dbg


def _finish(nc, p, dbg_ops, out_sts):
    keys = p.lower()
    with ExitStack() as sctx:
        sems = {k: sctx.enter_context(nc.semaphore(f"s_{k}")) for k in sorted(keys)}
        with nc.Block() as block:
            @block.tensor
            def _(t):
                p.emit_engine("T", t, sems)

            @block.vector
            def _(v):
                p.emit_engine("V", v, sems)

            @block.scalar
            def _(s):
                p.emit_engine("S", s, sems)

            @block.gpsimd
            def _(g):
                p.emit_engine("G", g, sems)

            @block.sync
            def _(sp):
                p.emit_engine("SP", sp, sems)
                if out_sts:
                    sp.wait_ge(sems["out"], p.final_counts["out"])
                if dbg_ops and "dbg" in sems:
                    sp.wait_ge(sems["dbg"], p.final_counts["dbg"])
    return nc



def _prep_inputs(L0_attn_X, L0_attn_Wq, L0_attn_bq, L0_attn_Wk, L0_attn_bk,
                 L0_attn_Wv, L0_attn_bv, L0_attn_Wo, L0_attn_bo,
                 L0_ffn_w_gating, L0_ffn_w_linear):
    X = np.ascontiguousarray(np.asarray(L0_attn_X, dtype=np.float32))
    Wq = np.asarray(L0_attn_Wq, dtype=np.float32)
    Wk = np.asarray(L0_attn_Wk, dtype=np.float32)
    Wv = np.asarray(L0_attn_Wv, dtype=np.float32)
    Wo = np.asarray(L0_attn_Wo, dtype=np.float32)
    wg = np.asarray(L0_ffn_w_gating, dtype=np.float32)
    wlin = np.asarray(L0_ffn_w_linear, dtype=np.float32)

    def head_tiles(W):
        t = W.reshape(H, 128, NC, 128)
        return np.ascontiguousarray(t.transpose(0, 3, 2, 1)).reshape(H, 128, D)

    wq4 = head_tiles(Wq)
    wk4 = head_tiles(Wk)
    wv4 = head_tiles(Wv)
    wo4 = head_tiles(Wo)

    def gate_tiles(Wg):
        t = Wg.reshape(NC, 128, NH, 128)
        return np.ascontiguousarray(t.transpose(2, 1, 0, 3)).reshape(NH, 128, D)

    wg0 = gate_tiles(wg[0])
    wg1 = gate_tiles(wg[1])
    wl_t = np.ascontiguousarray(wlin.reshape(NH, 128, D))

    bq_r = np.ascontiguousarray(np.asarray(L0_attn_bq, np.float32).reshape(H, 128).T)
    bk_r = np.ascontiguousarray(np.asarray(L0_attn_bk, np.float32).reshape(H, 128).T)
    bv_r = np.ascontiguousarray(np.asarray(L0_attn_bv, np.float32).reshape(H, 128).T)
    bo_r = np.ascontiguousarray(np.asarray(L0_attn_bo, np.float32).reshape(1, D))

    in_maps = []
    for c in range(8):
        b = c // 2
        odd = c % 2
        XT = np.ascontiguousarray(X[b].T)
        if odd:
            xt_c = XT
        else:
            xt_c = np.ascontiguousarray(
                np.concatenate([np.zeros((D, 128), np.float32), XT[:, :S - 128]], axis=1))
        pmb_c = np.full((128, 1), (0.0 if odd else -1e30), np.float32)
        in_maps.append({
            "xt": xt_c,
            "wq4": wq4, "wk4": wk4, "wv4": wv4, "wo4": wo4,
            "bq": bq_r, "bk": bk_r, "bv": bv_r, "bo": bo_r,
            "wg0": wg0, "wg1": wg1, "wl": wl_t,
            "pmb": pmb_c,
        })
    return in_maps


def _assemble(results):
    outf = np.empty((B, S, D), dtype=np.float32)
    for c in range(8):
        b = c // 2
        odd = c % 2
        oc = results[c]["out"]
        for i in range(8):
            t_true = 2 * i + odd
            outf[b, t_true * 128:(t_true + 1) * 128, :] = oc[:, i * 128:(i + 1) * 128].T
    return outf


_CACHED = {}


def kernel(**inputs):
    in_maps = _prep_inputs(**inputs)
    if "nc" not in _CACHED:
        _CACHED["nc"], _ = build_program()
    res = run_bass_kernel_spmd(_CACHED["nc"], in_maps, core_ids=list(range(8)))
    return _assemble(res.results)
